# version 01
# baseline (speedup 1.0000x reference)
import contextlib
import numpy as np
import concourse.bass as bass
import concourse.mybir as mybir
from concourse.bass_utils import run_bass_kernel_spmd

F32 = mybir.dt.float32
BF16 = mybir.dt.bfloat16
I8 = mybir.dt.int8
AF = mybir.ActivationFunctionType
ALU = mybir.AluOpType

D = 2048
NT = 16
SEQ = 4096
RK = 512
NCH = SEQ // RK
RF = 256
C_Q = float(128 ** 3)
C_KV = 128.0

_EK = ("t", "v", "s", "g", "q")


class Bld:
    def __init__(self, nc, block, semnames, es):
        self.nc = nc
        self.sec = {"t": block.tensor, "v": block.vector, "s": block.scalar,
                    "g": block.gpsimd, "q": block.sync}
        self.sem, self.cnt = {}, {}
        self.seen = {k: {} for k in _EK}
        for n in semnames:
            self.sem[n] = es.enter_context(nc.semaphore(n))
            self.cnt[n] = 0
        self.wr, self.rd = {}, {}

    def emit(self, ekey, deps, fn, inc):
        step = 16 if inc.startswith("D") else 1
        todo, seen = [], self.seen[ekey]
        for d in deps:
            if d is None:
                continue
            s, v = d
            if v > 0 and seen.get(s, 0) < v:
                todo.append((s, v))
                seen[s] = v
        semh, inc_sem = self.sem, self.sem[inc]

        def body(eng):
            for s, v in todo:
                eng.wait_ge(semh[s], v)
            fn(eng).then_inc(inc_sem, step)

        self.sec[ekey](body)
        self.cnt[inc] += step
        return (inc, self.cnt[inc])

    def dep_r(self, *bufs):
        return [self.wr.get(b) for b in bufs]

    def dep_w(self, *bufs):
        out = []
        for b in bufs:
            out.extend(self.rd.get(b, {}).items())
            if b in self.wr:
                out.append(self.wr[b])
        return out

    def did_r(self, mark, *bufs):
        for b in bufs:
            d = self.rd.setdefault(b, {})
            if d.get(mark[0], 0) < mark[1]:
                d[mark[0]] = mark[1]

    def did_w(self, mark, *bufs):
        for b in bufs:
            self.wr[b] = mark
            self.rd.pop(b, None)


def build_program():
    nc = bass.Bass("TRN2", target_bir_lowering=False)
    dti = lambda n, s, d: nc.dram_tensor(n, s, d, kind="ExternalInput").ap()
    xkv = dti("xkv", [D, SEQ], F32)
    wq = dti("wq", [16, 128, 16, 128], F32)
    wkv = dti("wkv", [16, 128, 16, 128], F32)
    wo = dti("wo", [16, 128, 16, 128], F32)
    wg = dti("wg", [64, 128, 16, 128], I8)
    wu = dti("wu", [64, 128, 16, 128], I8)
    wd = dti("wd", [16, 128, 64, 128], I8)
    sg = dti("sg", [128, 64], F32)
    su = dti("su", [128, 64], F32)
    sd = dti("sd", [128, 16], F32)
    cfm = dti("cf", [128, 128], F32)
    sfm = dti("sf", [128, 128], F32)
    mask = dti("mask", [16, SEQ], F32)
    out = nc.dram_tensor("out", [D, 1024], F32, kind="ExternalOutput").ap()
    nscr = nc.dram_tensor("nscr", [2, 16, RK], BF16, kind="Internal").ap()
    rnscr = nc.dram_tensor("rnscr", [2, 16, RK], BF16, kind="Internal").ap()
    ascr = nc.dram_tensor("ascr", [D, RK], BF16, kind="Internal").ap()

    sems = ["T", "V", "S", "G", "DC",
            "DX0", "DX1", "DW0", "DW1", "DG0", "DG1", "DU0", "DU1",
            "DD0", "DD1", "DN0", "DN1", "DR0", "DR1", "DA0", "DA1",
            "DB0", "DB1", "DM0", "DM1", "DO"]

    es = contextlib.ExitStack()
    with es:
        sb = lambda n, s, d: es.enter_context(nc.sbuf_tensor(n, s, d))
        xn_roll = sb("xn_roll", [128, NT * RK], BF16)
        xn_own = sb("xn_own", [128, NT * RK], BF16)
        kvb = sb("kvb", [128, NT * RK], BF16)
        kvc = [sb("kvc0", [128, NT * RK], BF16)]
        carry = [sb("carry0", [128, NT], F32), sb("carry1", [128, NT], F32)]
        ctmp = sb("ctmp", [128, NT], F32)
        sqt = [sb("sqt0", [128, RK], BF16), sb("sqt1", [128, RK], BF16)]
        nbuf = [sb("nbuf0", [1, RK], BF16), sb("nbuf1", [1, RK], BF16)]
        n16 = sb("n16", [16, RK], BF16)
        rn32 = sb("rn32", [16, RK], F32)
        rnm = sb("rnm", [16, RK], BF16)
        rnb = [sb("rnb0", [128, RK], BF16), sb("rnb1", [128, RK], BF16)]
        mv16 = sb("mv16", [1, RK], BF16)
        e2f = sb("e2f", [1, RK], F32)
        vtmp = sb("vtmp", [1, RK], F32)
        rstdf = sb("rstdf", [1, RK], F32)
        rstd16 = sb("rstd16", [1, RK], BF16)
        qsb = sb("qsb", [128, NT * RK], BF16)
        vh = sb("vh", [128, NT * RK], BF16)
        hb = [[sb(f"hb{p}_{i}", [128, RK], BF16) for i in range(6)] for p in range(2)]
        attn_st = [sb("ast0", [128, RK], BF16), sb("ast1", [128, RK], BF16)]
        wbuf = [sb("wb0", [128, 2048], BF16), sb("wb1", [128, 2048], BF16)]
        wgb = [sb("wgb0", [128, 2048], BF16), sb("wgb1", [128, 2048], BF16)]
        wub = [sb("wub0", [128, 2048], BF16), sb("wub1", [128, 2048], BF16)]
        wdb = [sb("wdb0", [128, 2048], BF16), sb("wdb1", [128, 2048], BF16)]
        fuse = sb("fuse", [128, 64 * RF], BF16)
        silg = [sb("silg0", [128, RF], F32), sb("silg1", [128, RF], F32)]
        sgv = sb("sgv", [128, 64], F32)
        suv = sb("suv", [128, 64], F32)
        sdv = sb("sdv", [128, 16], F32)
        cfb = sb("cfb", [128, 128], BF16)
        sfb = sb("sfb", [128, 128], BF16)
        onesM = sb("onesM", [128, 1], BF16)
        ones1 = sb("ones1", [1, 128], BF16)
        onesS = sb("onesS", [128, RK], BF16)
        maskb = [sb("mkb0", [16, RK], F32), sb("mkb1", [16, RK], F32)]
        a_rd = [sb("ard0", [128, RF], BF16), sb("ard1", [128, RF], BF16)]
        x_rd = [sb("xrd0", [128, RF], BF16), sb("xrd1", [128, RF], BF16)]
        axr = [sb("axr0", [128, RF], F32), sb("axr1", [128, RF], F32)]
        outst = [sb("ost0", [128, RF], F32), sb("ost1", [128, RF], F32)]
        P = [es.enter_context(nc.psum_tensor(f"P{i}", [128, 512], F32)) for i in range(8)]
        sbuf_left = nc.sbuf_bytes_remaining
        block = es.enter_context(nc.Block())
        B = Bld(nc, block, sems, es)
        B.sbuf_left = sbuf_left

        def prolog(g):
            g.memset(onesM[:], 1.0)
            g.memset(ones1[:], 1.0)
            g.memset(onesS[:], 1.0)
            g.memset(carry[0][:], 0.0)
            return g.memset(carry[1][:], 0.0)
        mk = B.emit("g", [], prolog, "G")
        B.did_w(mk, "onesM", "ones1", "onesS", "carry0", "carry1")
        for dst, src, nm in ((sgv, sg, "sgv"), (suv, su, "suv"), (sdv, sd, "sdv"),
                             (cfb, cfm, "cfb"), (sfb, sfm, "sfb")):
            mk = B.emit("g", [], (lambda d_, s_: (lambda g: g.dma_start(out=d_[:], in_=s_)))(dst, src), "DC")
            B.did_w(mk, nm)

        def dma_x_chunk(c, dst, xname):
            sem = "DX%d" % (c % 2)
            mk = B.emit("g", B.dep_w(xname),
                        lambda g, d_=dst, c_=c: g.dma_start(
                            out=d_[:, 0:NT * RK].rearrange("p (t r) -> p t r", t=NT),
                            in_=xkv[:, c_ * RK:(c_ + 1) * RK].rearrange("(t p) r -> p t r", p=128)),
                        sem)
            B.did_w(mk, xname, *[xname + "_t%d" % i for i in range(NT)])
            return mk

        def dma_mask(c):
            p = c % 2
            mk = B.emit("q", B.dep_w("mkb%d" % p),
                        lambda q, c_=c, p_=p: q.dma_start(out=maskb[p_][:], in_=mask[:, c_ * RK:(c_ + 1) * RK]),
                        "DM%d" % p)
            B.did_w(mk, "mkb%d" % p)

        def dma_wtile(wsrc, mt, bufs, bname, sem_base):
            par = mt % 2
            nm = bname + str(par)
            mk = B.emit("g", B.dep_w(nm),
                        lambda g, w_=wsrc, m_=mt, d_=bufs[par]: g.dma_start(
                            out=d_[:], in_=w_[m_].rearrange("p a b -> p (a b)")),
                        "%s%d" % (sem_base, par))
            B.did_w(mk, nm)

        def dma_wd_piece(idx, ot, pc):
            par = idx % 2
            nm = "wdb%d" % par
            mk = B.emit("g", B.dep_w(nm),
                        lambda g, o_=ot, p_=pc, d_=wdb[par]: g.dma_start(
                            out=d_[:], in_=wd[o_, :, p_ * 16:(p_ + 1) * 16, :].rearrange("p a b -> p (a b)")),
                        "DD%d" % par)
            B.did_w(mk, nm)

        def ln_chunk(c, xdst, xname, own):
            def mm_sum(t, x_=xdst):
                r = None
                for kt in range(NT):
                    r = t.matmul(P[2][0:1, :], onesM[:], x_[:, kt * RK:(kt + 1) * RK],
                                 start=(kt == 0), stop=(kt == NT - 1))
                return r
            mk = B.emit("t", B.dep_r(xname, "onesM") + B.dep_w("P2"), mm_sum, "T")
            B.did_w(mk, "P2"); B.did_r(mk, xname, "onesM")
            mk = B.emit("v", B.dep_r("P2") + B.dep_w("mv16"),
                        lambda v: v.tensor_scalar_mul(mv16[0:1, :], P[2][0:1, :], 1.0 / D), "V")
            B.did_w(mk, "mv16"); B.did_r(mk, "P2")
            if own:
                for kt in range(NT):
                    p = kt % 2
                    mk = B.emit("v", B.dep_r(xname) + B.dep_w("sqt%d" % p),
                                lambda v, k_=kt, p_=p, x_=xdst: v.tensor_tensor(
                                    sqt[p_][:], x_[:, k_ * RK:(k_ + 1) * RK],
                                    x_[:, k_ * RK:(k_ + 1) * RK], op=ALU.mult), "V")
                    B.did_w(mk, "sqt%d" % p); B.did_r(mk, xname)
                    mk2 = B.emit("t", B.dep_r("sqt%d" % p, "onesM") + (B.dep_w("P3") if kt == 0 else []),
                                 lambda t, k_=kt, p_=p: t.matmul(P[3][0:1, :], onesM[:], sqt[p_][:],
                                                                 start=(k_ == 0), stop=(k_ == NT - 1)), "T")
                    B.did_r(mk2, "sqt%d" % p, "onesM")
                    if kt == NT - 1:
                        B.did_w(mk2, "P3")
                mk = B.emit("v", B.dep_r("P3") + B.dep_w("e2f"),
                            lambda v: v.tensor_scalar_mul(e2f[0:1, :], P[3][0:1, :], 1.0 / D), "V")
                B.did_w(mk, "e2f"); B.did_r(mk, "P3")
                mk = B.emit("v", B.dep_r("mv16") + B.dep_w("vtmp"),
                            lambda v: v.tensor_tensor(vtmp[0:1, :], mv16[0:1, :], mv16[0:1, :], op=ALU.mult), "V")
                B.did_w(mk, "vtmp"); B.did_r(mk, "mv16")
                mk = B.emit("v", B.dep_r("e2f") + B.dep_w("vtmp"),
                            lambda v: v.tensor_tensor(vtmp[0:1, :], e2f[0:1, :], vtmp[0:1, :], op=ALU.subtract), "V")
                B.did_w(mk, "vtmp"); B.did_r(mk, "e2f")
                mk = B.emit("v", B.dep_r("vtmp"),
                            lambda v: v.tensor_scalar_add(vtmp[0:1, :], vtmp[0:1, :], 1e-5), "V")
                B.did_w(mk, "vtmp")
                mk = B.emit("s", B.dep_r("vtmp") + B.dep_w("rstdf"),
                            lambda s: s.activation(rstdf[0:1, :], vtmp[0:1, :], AF.Sqrt), "S")
                B.did_w(mk, "rstdf"); B.did_r(mk, "vtmp")
                mk = B.emit("v", B.dep_r("rstdf"),
                            lambda v: v.reciprocal(rstdf[0:1, :], rstdf[0:1, :]), "V")
                B.did_w(mk, "rstdf")
                mk = B.emit("v", B.dep_r("rstdf") + B.dep_w("rstd16"),
                            lambda v: v.tensor_copy(rstd16[0:1, :], rstdf[0:1, :]), "V")
                B.did_w(mk, "rstd16"); B.did_r(mk, "rstdf")
            mk = B.emit("t", B.dep_r("mv16", "ones1") + B.dep_w("P4"),
                        lambda t: t.matmul(P[4][:], ones1[:], mv16[0:1, :], start=True, stop=True), "T")
            B.did_w(mk, "P4"); B.did_r(mk, "mv16", "ones1")
            if own:
                mk = B.emit("t", B.dep_r("rstd16", "ones1") + B.dep_w("P5"),
                            lambda t: t.matmul(P[5][:], ones1[:], rstd16[0:1, :], start=True, stop=True), "T")
                B.did_w(mk, "P5"); B.did_r(mk, "rstd16", "ones1")
            for kt in range(NT):
                mk = B.emit("v", B.dep_r(xname, "P4"),
                            lambda v, k_=kt, x_=xdst: v.tensor_tensor(
                                x_[:, k_ * RK:(k_ + 1) * RK], x_[:, k_ * RK:(k_ + 1) * RK],
                                P[4][:], op=ALU.subtract), "V")
                B.did_w(mk, xname); B.did_r(mk, "P4")
                if own:
                    mk = B.emit("v", B.dep_r("P5"),
                                lambda v, k_=kt, x_=xdst: v.tensor_tensor(
                                    x_[:, k_ * RK:(k_ + 1) * RK], x_[:, k_ * RK:(k_ + 1) * RK],
                                    P[5][:], op=ALU.mult), "V")
                    B.did_w(mk, xname); B.did_r(mk, "P5")
            for i in range(NT):
                B.did_w(B.wr[xname], xname + "_t%d" % i)

        def proj(xsrc, xname, wsrc, dst, dstname, psA, psB):
            dma_wtile(wsrc, 0, wbuf, "wb", "DW")
            dma_wtile(wsrc, 1, wbuf, "wb", "DW")
            pps = [psA, psB]
            for mt in range(NT):
                par = mt % 2
                pp = P[pps[par]]
                pn = "P%d" % pps[par]
                def mm(t, p_=pp, w_=wbuf[par], x_=xsrc):
                    r = None
                    for kt in range(NT):
                        r = t.matmul(p_[:], w_[:, kt * 128:(kt + 1) * 128],
                                     x_[:, kt * RK:(kt + 1) * RK],
                                     start=(kt == 0), stop=(kt == NT - 1))
                    return r
                mk = B.emit("t", B.dep_r("wb%d" % par, xname) + B.dep_w(pn), mm, "T")
                B.did_w(mk, pn); B.did_r(mk, "wb%d" % par, xname)
                if mt + 2 < NT:
                    dma_wtile(wsrc, mt + 2, wbuf, "wb", "DW")
                mk = B.emit("s", B.dep_r(pn) + B.dep_w(dstname + "_t%d" % mt),
                            lambda s, m_=mt, p_=pp, d_=dst: s.copy(d_[:, m_ * RK:(m_ + 1) * RK], p_[:]), "S")
                B.did_r(mk, pn); B.did_w(mk, dstname + "_t%d" % mt)

        def norms(srcbuf, srcname, cpar, sq_scale, maskname):
            nsem = "DN%d" % cpar
            rsem = "DR%d" % cpar
            for mt in range(NT):
                p = mt % 2
                mk = B.emit("v", B.dep_r(srcname + "_t%d" % mt) + B.dep_w("sqt%d" % p),
                            lambda v, m_=mt, p_=p, x_=srcbuf: v.tensor_tensor(
                                sqt[p_][:], x_[:, m_ * RK:(m_ + 1) * RK],
                                x_[:, m_ * RK:(m_ + 1) * RK], op=ALU.mult), "V")
                B.did_w(mk, "sqt%d" % p); B.did_r(mk, srcname + "_t%d" % mt)
                pn = "P%d" % (6 + p)
                mk = B.emit("t", B.dep_r("sqt%d" % p, "onesM") + B.dep_w(pn),
                            lambda t, p_=p: t.matmul(P[6 + p_][0:1, :], onesM[:], sqt[p_][:],
                                                     start=True, stop=True), "T")
                B.did_w(mk, pn); B.did_r(mk, "sqt%d" % p, "onesM")
                mk = B.emit("s", B.dep_r(pn) + B.dep_w("nbuf%d" % p),
                            lambda s, p_=p: s.copy(nbuf[p_][0:1, :], P[6 + p_][0:1, :]), "S")
                B.did_r(mk, pn); B.did_w(mk, "nbuf%d" % p)
                mk = B.emit("q", B.dep_r("nbuf%d" % p) + B.dep_w("nscr%d" % cpar),
                            lambda q, m_=mt, p_=p, c_=cpar: q.dma_start(
                                out=nscr[c_, m_].unsqueeze(0), in_=nbuf[p_][0:1, :]), nsem)
                B.did_r(mk, "nbuf%d" % p)
                if mt == NT - 1:
                    B.did_w(mk, "nscr%d" % cpar)
                else:
                    B.did_w(mk, "nscr_part%d" % cpar)
            mk = B.emit("q", [(nsem, B.cnt[nsem])] + B.dep_w("n16"),
                        lambda q, c_=cpar: q.dma_start(out=n16[:], in_=nscr[c_]), nsem)
            B.did_w(mk, "n16"); B.did_r(mk, "nscr%d" % cpar)
            mk = B.emit("s", B.dep_r("n16") + B.dep_w("rn32"),
                        lambda s: s.activation(rn32[:], n16[:], AF.Sqrt, scale=float(sq_scale)), "S")
            B.did_w(mk, "rn32"); B.did_r(mk, "n16")
            mk = B.emit("v", B.dep_r("rn32"),
                        lambda v: v.reciprocal(rn32[:], rn32[:]), "V")
            B.did_w(mk, "rn32")
            if maskname is not None:
                mi = int(maskname[-1])
                mk = B.emit("v", B.dep_r("rn32", maskname) + B.dep_w("rnm"),
                            lambda v, m_=mi: v.tensor_tensor(rnm[:], rn32[:], maskb[m_][:], op=ALU.mult), "V")
                B.did_r(mk, "rn32", maskname)
            else:
                mk = B.emit("v", B.dep_r("rn32") + B.dep_w("rnm"),
                            lambda v: v.tensor_copy(rnm[:], rn32[:]), "V")
                B.did_r(mk, "rn32")
            B.did_w(mk, "rnm")
            mk = B.emit("q", B.dep_r("rnm") + B.dep_w("rnscr%d" % cpar),
                        lambda q, c_=cpar: q.dma_start(out=rnscr[c_], in_=rnm[:]), rsem)
            B.did_r(mk, "rnm"); B.did_w(mk, "rnscr%d" % cpar)
            for mt in range(NT):
                p = mt % 2
                mk = B.emit("g", B.dep_r("rnscr%d" % cpar) + B.dep_w("rnb%d" % p),
                            lambda g, m_=mt, p_=p, c_=cpar: g.dma_start(
                                out=rnb[p_][:], in_=rnscr[c_, m_].unsqueeze(0).broadcast_to((128, RK))), rsem)
                B.did_w(mk, "rnb%d" % p); B.did_r(mk, "rnscr%d" % cpar)
                mk = B.emit("v", B.dep_r("rnb%d" % p, srcname + "_t%d" % mt),
                            lambda v, m_=mt, p_=p, x_=srcbuf: v.tensor_tensor(
                                x_[:, m_ * RK:(m_ + 1) * RK], x_[:, m_ * RK:(m_ + 1) * RK],
                                rnb[p_][:], op=ALU.mult), "V")
                B.did_w(mk, srcname + "_t%d" % mt); B.did_r(mk, "rnb%d" % p)

        def k_chunk(c):
            own = c >= 6
            cpar = c % 2
            xdst = xn_own if own else xn_roll
            xname = "xn_own" if own else "xn_roll"
            dma_x_chunk(c, xdst, xname)
            dma_mask(c)
            ln_chunk(c, xdst, xname, own)
            proj(xdst, xname, wkv, kvb, "kvb", 0, 1)
            norms(kvb, "kvb", cpar, C_KV, "mkb%d" % cpar)
            cold, cnew = carry[c % 2], carry[1 - c % 2]
            coldn, cnewn = "carry%d" % (c % 2), "carry%d" % (1 - c % 2)
            for mt in range(NT):
                if own:
                    oc = c - 6
                    kname = "kvc%d" % oc
                    mk = B.emit("v", B.dep_r("kvb_t%d" % mt, coldn, "onesS") + B.dep_w(kname + "_t%d" % mt),
                                lambda v, m_=mt, d_=kvc[0], co_=cold: v.tensor_tensor_scan(
                                    d_[:, m_ * RK:(m_ + 1) * RK], onesS[:],
                                    kvb[:, m_ * RK:(m_ + 1) * RK], co_[:, m_:m_ + 1],
                                    op0=ALU.mult, op1=ALU.add), "V")
                    B.did_w(mk, kname + "_t%d" % mt)
                    B.did_r(mk, "kvb_t%d" % mt, coldn, "onesS")
                mk = B.emit("v", B.dep_r("kvb_t%d" % mt) + B.dep_w("ctmp"),
                            lambda v, m_=mt: v.tensor_reduce(
                                ctmp[:, m_:m_ + 1], kvb[:, m_ * RK:(m_ + 1) * RK],
                                axis=mybir.AxisListType.X, op=ALU.add), "V")
                B.did_w(mk, "ctmp"); B.did_r(mk, "kvb_t%d" % mt)
                mk = B.emit("v", B.dep_r("ctmp", coldn) + (B.dep_w(cnewn) if mt == 0 else []),
                            lambda v, m_=mt, co_=cold, cn_=cnew: v.tensor_tensor(
                                cn_[:, m_:m_ + 1], co_[:, m_:m_ + 1], ctmp[:, m_:m_ + 1], op=ALU.add), "V")
                B.did_r(mk, "ctmp", coldn)
                if mt == NT - 1:
                    B.did_w(mk, cnewn)

        def q_chunk(oc):
            cpar = oc
            kname = "kvc0"
            proj(xn_own, "xn_own", wq, qsb, "qsb", 0, 1)
            norms(qsb, "qsb", cpar, C_Q, None)
            for mt in range(NT):
                p = mt % 2
                names = [f"hb{p}_{i}" for i in range(6)]
                pa, pb = P[2 * p], P[2 * p + 1]
                pan, pbn = "P%d" % (2 * p), "P%d" % (2 * p + 1)
                for (src, srcn, mat, matn, dsti) in (
                        (qsb, "qsb_t%d" % mt, cfb, "cfb", 0),
                        (qsb, "qsb_t%d" % mt, sfb, "sfb", 1),
                        (kvc[0], kname + "_t%d" % mt, cfb, "cfb", 2),
                        (kvc[0], kname + "_t%d" % mt, sfb, "sfb", 3)):
                    pp, pn = (pa, pan) if dsti % 2 == 0 else (pb, pbn)
                    mk = B.emit("t", B.dep_r(srcn, matn) + B.dep_w(pn),
                                lambda t, s_=src, m_=mt, w_=mat, p_=pp: t.matmul(
                                    p_[:], w_[:], s_[:, m_ * RK:(m_ + 1) * RK], start=True, stop=True), "T")
                    B.did_w(mk, pn); B.did_r(mk, srcn, matn)
                    mk = B.emit("s", B.dep_r(pn) + B.dep_w(names[dsti]),
                                lambda s, b_=hb[p][dsti], p_=pp: s.copy(b_[:], p_[:]), "S")
                    B.did_r(mk, pn); B.did_w(mk, names[dsti])
                def prods(v, b_=hb[p]):
                    QC, QS, KC, KS, T5, T6 = b_
                    v.tensor_tensor(T5[:], QC[:], KC[:], op=ALU.mult)
                    v.tensor_tensor(T6[:], QS[:], KS[:], op=ALU.mult)
                    v.tensor_tensor(T5[:], T5[:], T6[:], op=ALU.subtract)
                    v.tensor_tensor(T6[:], QC[:], KS[:], op=ALU.mult)
                    v.tensor_tensor(QC[:], QS[:], KC[:], op=ALU.mult)
                    return v.tensor_tensor(T6[:], T6[:], QC[:], op=ALU.add)
                mk = B.emit("v", B.dep_r(*names) + B.dep_w(names[0], names[4], names[5]), prods, "V")
                B.did_r(mk, *names); B.did_w(mk, names[0], names[4], names[5])
                mk = B.emit("t", B.dep_r(names[4], names[5], "cfb", "sfb") + B.dep_w("P4"),
                            lambda t, b_=hb[p]: (t.matmul(P[4][:], cfb[:], b_[4][:], start=True, stop=False),
                                                 t.matmul(P[4][:], sfb[:], b_[5][:], start=False, stop=True))[-1], "T")
                B.did_w(mk, "P4"); B.did_r(mk, names[4], names[5], "cfb", "sfb")
                mk = B.emit("s", B.dep_r("P4") + B.dep_w("vh_t%d" % mt),
                            lambda s, m_=mt: s.copy(vh[:, m_ * RK:(m_ + 1) * RK], P[4][:]), "S")
                B.did_r(mk, "P4"); B.did_w(mk, "vh_t%d" % mt)
            vh_names = ["vh_t%d" % k for k in range(NT)]
            dma_wtile(wo, 0, wbuf, "wb", "DW")
            dma_wtile(wo, 1, wbuf, "wb", "DW")
            for mt in range(NT):
                par = mt % 2
                pp, pn = P[6 + par], "P%d" % (6 + par)
                def mmo(t, p_=pp, w_=wbuf[par]):
                    r = None
                    for kt in range(NT):
                        r = t.matmul(p_[:], w_[:, kt * 128:(kt + 1) * 128],
                                     vh[:, kt * RK:(kt + 1) * RK], start=(kt == 0), stop=(kt == NT - 1))
                    return r
                mk = B.emit("t", B.dep_r("wb%d" % par, *vh_names) + B.dep_w(pn), mmo, "T")
                B.did_w(mk, pn); B.did_r(mk, "wb%d" % par, *vh_names)
                if mt + 2 < NT:
                    dma_wtile(wo, mt + 2, wbuf, "wb", "DW")
                mk = B.emit("s", B.dep_r(pn) + B.dep_w("ast%d" % par),
                            lambda s, p_=pp, a_=attn_st[par]: s.copy(a_[:], p_[:]), "S")
                B.did_r(mk, pn); B.did_w(mk, "ast%d" % par)
                mk = B.emit("q", B.dep_r("ast%d" % par) + B.dep_w("ascr_t%d" % mt),
                            lambda q, m_=mt, a_=attn_st[par]: q.dma_start(
                                out=ascr[m_ * 128:(m_ + 1) * 128, :], in_=a_[:]), "DA%d" % par)
                B.did_r(mk, "ast%d" % par); B.did_w(mk, "ascr_t%d" % mt)

        def f_chunk(oc, sub):
            cs = sub * RF
            outcol = oc * RK + cs
            dma_wtile(wg, 0, wgb, "wgb", "DG")
            dma_wtile(wu, 0, wub, "wub", "DU")
            dma_wtile(wg, 1, wgb, "wgb", "DG")
            dma_wtile(wu, 1, wub, "wub", "DU")
            for mt in range(64):
                par = mt % 2
                pg, pgn = P[par], "P%d" % par
                pu, pun = P[2 + par], "P%d" % (2 + par)
                def mmg(t, p_=pg, w_=wgb[par]):
                    r = None
                    for kt in range(NT):
                        r = t.matmul(p_[:, 0:RF], w_[:, kt * 128:(kt + 1) * 128],
                                     xn_own[:, kt * RK + cs: kt * RK + cs + RF],
                                     start=(kt == 0), stop=(kt == NT - 1))
                    return r
                mk = B.emit("t", B.dep_r("wgb%d" % par, "xn_own") + B.dep_w(pgn), mmg, "T")
                B.did_w(mk, pgn); B.did_r(mk, "wgb%d" % par, "xn_own")
                if mt + 2 < 64:
                    dma_wtile(wg, mt + 2, wgb, "wgb", "DG")
                def mmu(t, p_=pu, w_=wub[par]):
                    r = None
                    for kt in range(NT):
                        r = t.matmul(p_[:, 0:RF], w_[:, kt * 128:(kt + 1) * 128],
                                     xn_own[:, kt * RK + cs: kt * RK + cs + RF],
                                     start=(kt == 0), stop=(kt == NT - 1))
                    return r
                mk = B.emit("t", B.dep_r("wub%d" % par, "xn_own") + B.dep_w(pun), mmu, "T")
                B.did_w(mk, pun); B.did_r(mk, "wub%d" % par, "xn_own")
                if mt + 2 < 64:
                    dma_wtile(wu, mt + 2, wub, "wub", "DU")
                mk = B.emit("s", B.dep_r(pgn, "sgv") + B.dep_w("silg%d" % par),
                            lambda s, m_=mt, p_=pg, g_=silg[par]: s.activation(
                                g_[:], p_[:, 0:RF], AF.Silu, scale=sgv[:, m_:m_ + 1]), "S")
                B.did_r(mk, pgn, "sgv"); B.did_w(mk, "silg%d" % par)
                mk = B.emit("v", B.dep_r(pun, "suv", "silg%d" % par) + B.dep_w("fuse_t%d" % mt),
                            lambda v, m_=mt, p_=pu, g_=silg[par]: v.scalar_tensor_tensor(
                                fuse[:, m_ * RF:(m_ + 1) * RF], p_[:, 0:RF], suv[:, m_:m_ + 1], g_[:],
                                op0=ALU.mult, op1=ALU.mult), "V")
                B.did_r(mk, pun, "suv", "silg%d" % par); B.did_w(mk, "fuse_t%d" % mt)
            dma_wd_piece(0, 0, 0)
            dma_wd_piece(1, 0, 1)
            fuse_names = ["fuse_t%d" % i for i in range(64)]
            for ot in range(NT):
                par = ot % 2
                pd, pdn = P[4 + par], "P%d" % (4 + par)
                mk = B.emit("q", B.dep_w("ard%d" % par) + B.dep_r("ascr_t%d" % ot),
                            lambda q, o_=ot, a_=a_rd[par]: q.dma_start(
                                out=a_[:], in_=ascr[o_ * 128:(o_ + 1) * 128, cs:cs + RF]), "DB%d" % par)
                B.did_w(mk, "ard%d" % par); B.did_r(mk, "ascr_t%d" % ot)
                mk = B.emit("g", B.dep_w("xrd%d" % par),
                            lambda g, o_=ot, x_=x_rd[par]: g.dma_start(
                                out=x_[:], in_=xkv[o_ * 128:(o_ + 1) * 128, 3072 + outcol: 3072 + outcol + RF]),
                            "DB%d" % par)
                B.did_w(mk, "xrd%d" % par)
                mk = B.emit("v", B.dep_r("ard%d" % par, "xrd%d" % par) + B.dep_w("axr%d" % par),
                            lambda v, a_=a_rd[par], x_=x_rd[par], r_=axr[par]: v.tensor_tensor(
                                r_[:], a_[:], x_[:], op=ALU.add), "V")
                B.did_r(mk, "ard%d" % par, "xrd%d" % par); B.did_w(mk, "axr%d" % par)
                for pc in range(4):
                    idx = ot * 4 + pc
                    dpar = idx % 2
                    deps = (B.dep_r("wdb%d" % dpar) + [B.dep_r(f)[0] for f in fuse_names[pc * 16:(pc + 1) * 16]] +
                            (B.dep_w(pdn) if pc == 0 else []))
                    def mmd(t, p_=pd, w_=wdb[dpar], pc_=pc):
                        r = None
                        for i in range(16):
                            kt = pc_ * 16 + i
                            r = t.matmul(p_[:, 0:RF], w_[:, i * 128:(i + 1) * 128],
                                         fuse[:, kt * RF:(kt + 1) * RF],
                                         start=(kt == 0), stop=(kt == 63))
                        return r
                    mk = B.emit("t", deps, mmd, "T")
                    B.did_r(mk, "wdb%d" % dpar, *fuse_names[pc * 16:(pc + 1) * 16])
                    if pc == 3:
                        B.did_w(mk, pdn)
                    if idx + 2 < 64:
                        nxt = idx + 2
                        dma_wd_piece(nxt, nxt // 4, nxt % 4)
                mk = B.emit("v", B.dep_r(pdn, "sdv", "axr%d" % par) + B.dep_w("ost%d" % par),
                            lambda v, o_=ot, p_=pd, r_=axr[par], u_=outst[par]: v.scalar_tensor_tensor(
                                u_[:], p_[:, 0:RF], sdv[:, o_:o_ + 1], r_[:], op0=ALU.mult, op1=ALU.add), "V")
                B.did_r(mk, pdn, "sdv", "axr%d" % par); B.did_w(mk, "ost%d" % par)
                mk = B.emit("q", B.dep_r("ost%d" % par),
                            lambda q, o_=ot, u_=outst[par]: q.dma_start(
                                out=out[o_ * 128:(o_ + 1) * 128, outcol:outcol + RF], in_=u_[:]), "DO")
                B.did_r(mk, "ost%d" % par)

        for c in range(6):
            k_chunk(c)
        k_chunk(6)
        q_chunk(0)
        f_chunk(0, 0)
        f_chunk(0, 1)
        k_chunk(7)
        q_chunk(1)
        f_chunk(1, 0)
        f_chunk(1, 1)
        B.emit("q", [("DO", B.cnt["DO"])], lambda q: q.nop(), "G")
    return nc


_NC_CACHE = None


def kernel(L0_attn_x, L0_attn_W_q, L0_attn_W_kv, L0_attn_W_o,
           L0_ffn_w_gate, L0_ffn_w_up, L0_ffn_w_down,
           L0_ffn_s_gate, L0_ffn_s_up, L0_ffn_s_down):
    x = np.asarray(L0_attn_x, dtype=np.float32)
    Wq = np.asarray(L0_attn_W_q, dtype=np.float32)
    Wkv = np.asarray(L0_attn_W_kv, dtype=np.float32)
    Wo = np.asarray(L0_attn_W_o, dtype=np.float32)
    wg = np.asarray(L0_ffn_w_gate, dtype=np.int8)
    wu = np.asarray(L0_ffn_w_up, dtype=np.int8)
    wd = np.asarray(L0_ffn_w_down, dtype=np.int8)
    sg = np.asarray(L0_ffn_s_gate, dtype=np.float32)
    su = np.asarray(L0_ffn_s_up, dtype=np.float32)
    sdn = np.asarray(L0_ffn_s_down, dtype=np.float32)

    t4a = lambda W: np.ascontiguousarray(W.reshape(16, 128, 16, 128).transpose(0, 3, 2, 1))
    wq_t, wkv_t, wo_t = t4a(Wq), t4a(Wkv), t4a(Wo)
    wg_t = np.ascontiguousarray(wg.reshape(16, 128, 64, 128).transpose(2, 1, 0, 3))
    wu_t = np.ascontiguousarray(wu.reshape(16, 128, 64, 128).transpose(2, 1, 0, 3))
    wd_t = np.ascontiguousarray(wd.reshape(64, 128, 16, 128).transpose(2, 1, 0, 3))
    sg_t = np.ascontiguousarray(sg.reshape(64, 128).T)
    su_t = np.ascontiguousarray(su.reshape(64, 128).T)
    sd_t = np.ascontiguousarray(sdn.reshape(16, 128).T)
    n = np.arange(128)
    ang = 2.0 * np.pi * np.outer(n, n) / 128.0
    cf = np.cos(ang).astype(np.float32)
    sf = np.sin(ang).astype(np.float32)

    in_maps = []
    for c in range(8):
        b, j = c // 4, c % 4
        xb = x[b]
        rolled = np.concatenate([xb[0:1024]] * (3 - j) + [xb[0:(j + 1) * 1024]], axis=0)
        x_kv = np.ascontiguousarray(rolled.T)
        m = np.zeros((16, SEQ), dtype=np.float32)
        m[:, (3 - j) * 1024:] = 1.0
        in_maps.append({
            "xkv": x_kv, "wq": wq_t, "wkv": wkv_t, "wo": wo_t,
            "wg": wg_t, "wu": wu_t, "wd": wd_t,
            "sg": sg_t, "su": su_t, "sd": sd_t,
            "cf": cf, "sf": sf, "mask": m,
        })

    global _NC_CACHE
    if _NC_CACHE is None:
        _NC_CACHE = build_program()
    res = run_bass_kernel_spmd(_NC_CACHE, in_maps, core_ids=list(range(8)))
    outp = np.empty((2, SEQ, D), dtype=np.float32)
    for c in range(8):
        b, j = c // 4, c % 4
        outp[b, j * 1024:(j + 1) * 1024, :] = np.asarray(res.results[c]["out"]).T
    return outp


# version 02
# speedup vs baseline: 1.1523x; 1.1523x over previous
import contextlib
import numpy as np
import concourse.bass as bass
import concourse.mybir as mybir
from concourse.bass_utils import run_bass_kernel_spmd

F32 = mybir.dt.float32
BF16 = mybir.dt.bfloat16
I8 = mybir.dt.int8
AF = mybir.ActivationFunctionType
ALU = mybir.AluOpType

D = 2048
NT = 16
SEQ = 4096
RK = 512
NCH = SEQ // RK
RF = 256
C_Q = float(128 ** 3)
C_KV = 128.0

_EK = ("t", "v", "s", "g", "q")


class Bld:
    def __init__(self, nc, block, semnames, es):
        self.nc = nc
        self.sec = {"t": block.tensor, "v": block.vector, "s": block.scalar,
                    "g": block.gpsimd, "q": block.sync}
        self.sem, self.cnt = {}, {}
        self.seen = {k: {} for k in _EK}
        for n in semnames:
            self.sem[n] = es.enter_context(nc.semaphore(n))
            self.cnt[n] = 0
        self.wr, self.rd = {}, {}
        self.pend = {}

    def emit(self, ekey, deps, fn, inc):
        step = 16 if inc.startswith("D") else 1
        todo, seen = [], self.seen[ekey]
        for d in deps:
            if d is None:
                continue
            s, v = d
            if v > 0 and seen.get(s, 0) < v:
                todo.append((s, v))
                seen[s] = v
        semh, inc_sem = self.sem, self.sem[inc]

        def body(eng):
            for s, v in todo:
                eng.wait_ge(semh[s], v)
            fn(eng).then_inc(inc_sem, step)

        self.pend.setdefault(ekey, []).append(body)
        self.cnt[inc] += step
        return (inc, self.cnt[inc])

    def flush(self):
        for ekey, bodies in self.pend.items():
            def run_all(eng, bodies_=bodies):
                for b in bodies_:
                    b(eng)
            self.sec[ekey](run_all)
        self.pend = {}

    def dep_r(self, *bufs):
        return [self.wr.get(b) for b in bufs]

    def dep_w(self, *bufs):
        out = []
        for b in bufs:
            out.extend(self.rd.get(b, {}).items())
            if b in self.wr:
                out.append(self.wr[b])
        return out

    def did_r(self, mark, *bufs):
        for b in bufs:
            d = self.rd.setdefault(b, {})
            if d.get(mark[0], 0) < mark[1]:
                d[mark[0]] = mark[1]

    def did_w(self, mark, *bufs):
        for b in bufs:
            self.wr[b] = mark
            self.rd.pop(b, None)


def build_program():
    nc = bass.Bass("TRN2", target_bir_lowering=False)
    dti = lambda n, s, d: nc.dram_tensor(n, s, d, kind="ExternalInput").ap()
    xkv = dti("xkv", [D, SEQ], F32)
    wq = dti("wq", [16, 128, 16, 128], F32)
    wkv = dti("wkv", [16, 128, 16, 128], F32)
    wo = dti("wo", [16, 128, 16, 128], F32)
    wg = dti("wg", [64, 128, 16, 128], I8)
    wu = dti("wu", [64, 128, 16, 128], I8)
    wd = dti("wd", [16, 128, 64, 128], I8)
    sg = dti("sg", [128, 64], F32)
    su = dti("su", [128, 64], F32)
    sd = dti("sd", [128, 16], F32)
    cfm = dti("cf", [128, 128], F32)
    sfm = dti("sf", [128, 128], F32)
    mask = dti("mask", [16, SEQ], F32)
    out = nc.dram_tensor("out", [D, 1024], F32, kind="ExternalOutput").ap()
    nscr = nc.dram_tensor("nscr", [2, 16, RK], BF16, kind="Internal").ap()
    rnscr = nc.dram_tensor("rnscr", [2, 16, RK], BF16, kind="Internal").ap()
    ascr = nc.dram_tensor("ascr", [D, RK], BF16, kind="Internal").ap()

    sems = ["T", "V", "S", "G", "DC",
            "DX0", "DX1", "DW0", "DW1", "DG0", "DG1", "DU0", "DU1",
            "DD0", "DD1", "DN0", "DN1", "DR0", "DR1", "DA0", "DA1",
            "DB0", "DB1", "DM0", "DM1", "DO"]

    es = contextlib.ExitStack()
    with es:
        sb = lambda n, s, d: es.enter_context(nc.sbuf_tensor(n, s, d))
        xn_roll = sb("xn_roll", [128, NT * RK], BF16)
        xn_own = sb("xn_own", [128, NT * RK], BF16)
        kvb = sb("kvb", [128, NT * RK], BF16)
        kvc = [sb("kvc0", [128, NT * RK], BF16)]
        carry = [sb("carry0", [128, NT], F32), sb("carry1", [128, NT], F32)]
        ctmp = sb("ctmp", [128, NT], F32)
        sqt = [sb("sqt0", [128, RK], BF16), sb("sqt1", [128, RK], BF16)]
        nbuf = [sb("nbuf0", [1, RK], BF16), sb("nbuf1", [1, RK], BF16)]
        n16 = sb("n16", [16, RK], BF16)
        rn32 = sb("rn32", [16, RK], F32)
        rnm = sb("rnm", [16, RK], BF16)
        rnb = [sb("rnb0", [128, RK], BF16), sb("rnb1", [128, RK], BF16)]
        mv16 = sb("mv16", [1, RK], BF16)
        e2f = sb("e2f", [1, RK], F32)
        vtmp = sb("vtmp", [1, RK], F32)
        rstdf = sb("rstdf", [1, RK], F32)
        rstd16 = sb("rstd16", [1, RK], BF16)
        qsb = sb("qsb", [128, NT * RK], BF16)
        vh = sb("vh", [128, NT * RK], BF16)
        hb = [[sb(f"hb{p}_{i}", [128, RK], BF16) for i in range(6)] for p in range(2)]
        attn_st = [sb("ast0", [128, RK], BF16), sb("ast1", [128, RK], BF16)]
        wbuf = [sb("wb0", [128, 2048], BF16), sb("wb1", [128, 2048], BF16)]
        wgb = [sb("wgb0", [128, 2048], BF16), sb("wgb1", [128, 2048], BF16)]
        wub = [sb("wub0", [128, 2048], BF16), sb("wub1", [128, 2048], BF16)]
        wdb = [sb("wdb0", [128, 2048], BF16), sb("wdb1", [128, 2048], BF16)]
        fuse = sb("fuse", [128, 64 * RF], BF16)
        silg = [sb("silg0", [128, RF], F32), sb("silg1", [128, RF], F32)]
        sgv = sb("sgv", [128, 64], F32)
        suv = sb("suv", [128, 64], F32)
        sdv = sb("sdv", [128, 16], F32)
        cfb = sb("cfb", [128, 128], BF16)
        sfb = sb("sfb", [128, 128], BF16)
        onesM = sb("onesM", [128, 1], BF16)
        ones1 = sb("ones1", [1, 128], BF16)
        onesS = sb("onesS", [128, RK], BF16)
        maskb = [sb("mkb0", [16, RK], F32), sb("mkb1", [16, RK], F32)]
        a_rd = [sb("ard0", [128, RF], BF16), sb("ard1", [128, RF], BF16)]
        x_rd = [sb("xrd0", [128, RF], BF16), sb("xrd1", [128, RF], BF16)]
        axr = [sb("axr0", [128, RF], F32), sb("axr1", [128, RF], F32)]
        outst = [sb("ost0", [128, RF], F32), sb("ost1", [128, RF], F32)]
        P = [es.enter_context(nc.psum_tensor(f"P{i}", [128, 512], F32)) for i in range(8)]
        sbuf_left = nc.sbuf_bytes_remaining
        block = es.enter_context(nc.Block())
        B = Bld(nc, block, sems, es)
        B.sbuf_left = sbuf_left

        def prolog(g):
            g.memset(onesM[:], 1.0)
            g.memset(ones1[:], 1.0)
            g.memset(onesS[:], 1.0)
            g.memset(carry[0][:], 0.0)
            return g.memset(carry[1][:], 0.0)
        mk = B.emit("g", [], prolog, "G")
        B.did_w(mk, "onesM", "ones1", "onesS", "carry0", "carry1")
        for dst, src, nm in ((sgv, sg, "sgv"), (suv, su, "suv"), (sdv, sd, "sdv"),
                             (cfb, cfm, "cfb"), (sfb, sfm, "sfb")):
            mk = B.emit("g", [], (lambda d_, s_: (lambda g: g.dma_start(out=d_[:], in_=s_)))(dst, src), "DC")
            B.did_w(mk, nm)

        def dma_x_chunk(c, dst, xname):
            sem = "DX%d" % (c % 2)
            mk = B.emit("g", B.dep_w(xname),
                        lambda g, d_=dst, c_=c: g.dma_start(
                            out=d_[:, 0:NT * RK].rearrange("p (t r) -> p t r", t=NT),
                            in_=xkv[:, c_ * RK:(c_ + 1) * RK].rearrange("(t p) r -> p t r", p=128)),
                        sem)
            B.did_w(mk, xname, *[xname + "_t%d" % i for i in range(NT)])
            return mk

        def dma_mask(c):
            p = c % 2
            mk = B.emit("q", B.dep_w("mkb%d" % p),
                        lambda q, c_=c, p_=p: q.dma_start(out=maskb[p_][:], in_=mask[:, c_ * RK:(c_ + 1) * RK]),
                        "DM%d" % p)
            B.did_w(mk, "mkb%d" % p)

        def dma_wtile(wsrc, mt, bufs, bname, sem_base):
            par = mt % 2
            nm = bname + str(par)
            mk = B.emit("g", B.dep_w(nm),
                        lambda g, w_=wsrc, m_=mt, d_=bufs[par]: g.dma_start(
                            out=d_[:], in_=w_[m_].rearrange("p a b -> p (a b)")),
                        "%s%d" % (sem_base, par))
            B.did_w(mk, nm)

        def dma_wd_piece(idx, ot, pc):
            par = idx % 2
            nm = "wdb%d" % par
            mk = B.emit("g", B.dep_w(nm),
                        lambda g, o_=ot, p_=pc, d_=wdb[par]: g.dma_start(
                            out=d_[:], in_=wd[o_, :, p_ * 16:(p_ + 1) * 16, :].rearrange("p a b -> p (a b)")),
                        "DD%d" % par)
            B.did_w(mk, nm)

        def ln_chunk(c, xdst, xname, own):
            def mm_sum(t, x_=xdst):
                r = None
                for kt in range(NT):
                    r = t.matmul(P[2][0:1, :], onesM[:], x_[:, kt * RK:(kt + 1) * RK],
                                 start=(kt == 0), stop=(kt == NT - 1))
                return r
            mk = B.emit("t", B.dep_r(xname, "onesM") + B.dep_w("P2"), mm_sum, "T")
            B.did_w(mk, "P2"); B.did_r(mk, xname, "onesM")
            mk = B.emit("v", B.dep_r("P2") + B.dep_w("mv16"),
                        lambda v: v.tensor_scalar_mul(mv16[0:1, :], P[2][0:1, :], 1.0 / D), "V")
            B.did_w(mk, "mv16"); B.did_r(mk, "P2")
            if own:
                for kt in range(NT):
                    p = kt % 2
                    mk = B.emit("v", B.dep_r(xname) + B.dep_w("sqt%d" % p),
                                lambda v, k_=kt, p_=p, x_=xdst: v.tensor_tensor(
                                    sqt[p_][:], x_[:, k_ * RK:(k_ + 1) * RK],
                                    x_[:, k_ * RK:(k_ + 1) * RK], op=ALU.mult), "V")
                    B.did_w(mk, "sqt%d" % p); B.did_r(mk, xname)
                    mk2 = B.emit("t", B.dep_r("sqt%d" % p, "onesM") + (B.dep_w("P3") if kt == 0 else []),
                                 lambda t, k_=kt, p_=p: t.matmul(P[3][0:1, :], onesM[:], sqt[p_][:],
                                                                 start=(k_ == 0), stop=(k_ == NT - 1)), "T")
                    B.did_r(mk2, "sqt%d" % p, "onesM")
                    if kt == NT - 1:
                        B.did_w(mk2, "P3")
                mk = B.emit("v", B.dep_r("P3") + B.dep_w("e2f"),
                            lambda v: v.tensor_scalar_mul(e2f[0:1, :], P[3][0:1, :], 1.0 / D), "V")
                B.did_w(mk, "e2f"); B.did_r(mk, "P3")
                mk = B.emit("v", B.dep_r("mv16") + B.dep_w("vtmp"),
                            lambda v: v.tensor_tensor(vtmp[0:1, :], mv16[0:1, :], mv16[0:1, :], op=ALU.mult), "V")
                B.did_w(mk, "vtmp"); B.did_r(mk, "mv16")
                mk = B.emit("v", B.dep_r("e2f") + B.dep_w("vtmp"),
                            lambda v: v.tensor_tensor(vtmp[0:1, :], e2f[0:1, :], vtmp[0:1, :], op=ALU.subtract), "V")
                B.did_w(mk, "vtmp"); B.did_r(mk, "e2f")
                mk = B.emit("v", B.dep_r("vtmp"),
                            lambda v: v.tensor_scalar_add(vtmp[0:1, :], vtmp[0:1, :], 1e-5), "V")
                B.did_w(mk, "vtmp")
                mk = B.emit("s", B.dep_r("vtmp") + B.dep_w("rstdf"),
                            lambda s: s.activation(rstdf[0:1, :], vtmp[0:1, :], AF.Sqrt), "S")
                B.did_w(mk, "rstdf"); B.did_r(mk, "vtmp")
                mk = B.emit("v", B.dep_r("rstdf"),
                            lambda v: v.reciprocal(rstdf[0:1, :], rstdf[0:1, :]), "V")
                B.did_w(mk, "rstdf")
                mk = B.emit("v", B.dep_r("rstdf") + B.dep_w("rstd16"),
                            lambda v: v.tensor_copy(rstd16[0:1, :], rstdf[0:1, :]), "V")
                B.did_w(mk, "rstd16"); B.did_r(mk, "rstdf")
            mk = B.emit("t", B.dep_r("mv16", "ones1") + B.dep_w("P4"),
                        lambda t: t.matmul(P[4][:], ones1[:], mv16[0:1, :], start=True, stop=True), "T")
            B.did_w(mk, "P4"); B.did_r(mk, "mv16", "ones1")
            if own:
                mk = B.emit("t", B.dep_r("rstd16", "ones1") + B.dep_w("P5"),
                            lambda t: t.matmul(P[5][:], ones1[:], rstd16[0:1, :], start=True, stop=True), "T")
                B.did_w(mk, "P5"); B.did_r(mk, "rstd16", "ones1")
            for kt in range(NT):
                mk = B.emit("v", B.dep_r(xname, "P4"),
                            lambda v, k_=kt, x_=xdst: v.tensor_tensor(
                                x_[:, k_ * RK:(k_ + 1) * RK], x_[:, k_ * RK:(k_ + 1) * RK],
                                P[4][:], op=ALU.subtract), "V")
                B.did_w(mk, xname); B.did_r(mk, "P4")
                if own:
                    mk = B.emit("v", B.dep_r("P5"),
                                lambda v, k_=kt, x_=xdst: v.tensor_tensor(
                                    x_[:, k_ * RK:(k_ + 1) * RK], x_[:, k_ * RK:(k_ + 1) * RK],
                                    P[5][:], op=ALU.mult), "V")
                    B.did_w(mk, xname); B.did_r(mk, "P5")
            for i in range(NT):
                B.did_w(B.wr[xname], xname + "_t%d" % i)

        def proj(xsrc, xname, wsrc, dst, dstname, psA, psB):
            dma_wtile(wsrc, 0, wbuf, "wb", "DW")
            dma_wtile(wsrc, 1, wbuf, "wb", "DW")
            pps = [psA, psB]
            for mt in range(NT):
                par = mt % 2
                pp = P[pps[par]]
                pn = "P%d" % pps[par]
                def mm(t, p_=pp, w_=wbuf[par], x_=xsrc):
                    r = None
                    for kt in range(NT):
                        r = t.matmul(p_[:], w_[:, kt * 128:(kt + 1) * 128],
                                     x_[:, kt * RK:(kt + 1) * RK],
                                     start=(kt == 0), stop=(kt == NT - 1))
                    return r
                mk = B.emit("t", B.dep_r("wb%d" % par, xname) + B.dep_w(pn), mm, "T")
                B.did_w(mk, pn); B.did_r(mk, "wb%d" % par, xname)
                if mt + 2 < NT:
                    dma_wtile(wsrc, mt + 2, wbuf, "wb", "DW")
                mk = B.emit("s", B.dep_r(pn) + B.dep_w(dstname + "_t%d" % mt),
                            lambda s, m_=mt, p_=pp, d_=dst: s.copy(d_[:, m_ * RK:(m_ + 1) * RK], p_[:]), "S")
                B.did_r(mk, pn); B.did_w(mk, dstname + "_t%d" % mt)

        def norms(srcbuf, srcname, cpar, sq_scale, maskname):
            nsem = "DN%d" % cpar
            rsem = "DR%d" % cpar
            for mt in range(NT):
                p = mt % 2
                mk = B.emit("v", B.dep_r(srcname + "_t%d" % mt) + B.dep_w("sqt%d" % p),
                            lambda v, m_=mt, p_=p, x_=srcbuf: v.tensor_tensor(
                                sqt[p_][:], x_[:, m_ * RK:(m_ + 1) * RK],
                                x_[:, m_ * RK:(m_ + 1) * RK], op=ALU.mult), "V")
                B.did_w(mk, "sqt%d" % p); B.did_r(mk, srcname + "_t%d" % mt)
                pn = "P%d" % (6 + p)
                mk = B.emit("t", B.dep_r("sqt%d" % p, "onesM") + B.dep_w(pn),
                            lambda t, p_=p: t.matmul(P[6 + p_][0:1, :], onesM[:], sqt[p_][:],
                                                     start=True, stop=True), "T")
                B.did_w(mk, pn); B.did_r(mk, "sqt%d" % p, "onesM")
                mk = B.emit("s", B.dep_r(pn) + B.dep_w("nbuf%d" % p),
                            lambda s, p_=p: s.copy(nbuf[p_][0:1, :], P[6 + p_][0:1, :]), "S")
                B.did_r(mk, pn); B.did_w(mk, "nbuf%d" % p)
                mk = B.emit("q", B.dep_r("nbuf%d" % p) + B.dep_w("nscr%d" % cpar),
                            lambda q, m_=mt, p_=p, c_=cpar: q.dma_start(
                                out=nscr[c_, m_].unsqueeze(0), in_=nbuf[p_][0:1, :]), nsem)
                B.did_r(mk, "nbuf%d" % p)
                if mt == NT - 1:
                    B.did_w(mk, "nscr%d" % cpar)
                else:
                    B.did_w(mk, "nscr_part%d" % cpar)
            mk = B.emit("q", [(nsem, B.cnt[nsem])] + B.dep_w("n16"),
                        lambda q, c_=cpar: q.dma_start(out=n16[:], in_=nscr[c_]), nsem)
            B.did_w(mk, "n16"); B.did_r(mk, "nscr%d" % cpar)
            mk = B.emit("s", B.dep_r("n16") + B.dep_w("rn32"),
                        lambda s: s.activation(rn32[:], n16[:], AF.Sqrt, scale=float(sq_scale)), "S")
            B.did_w(mk, "rn32"); B.did_r(mk, "n16")
            mk = B.emit("v", B.dep_r("rn32"),
                        lambda v: v.reciprocal(rn32[:], rn32[:]), "V")
            B.did_w(mk, "rn32")
            if maskname is not None:
                mi = int(maskname[-1])
                mk = B.emit("v", B.dep_r("rn32", maskname) + B.dep_w("rnm"),
                            lambda v, m_=mi: v.tensor_tensor(rnm[:], rn32[:], maskb[m_][:], op=ALU.mult), "V")
                B.did_r(mk, "rn32", maskname)
            else:
                mk = B.emit("v", B.dep_r("rn32") + B.dep_w("rnm"),
                            lambda v: v.tensor_copy(rnm[:], rn32[:]), "V")
                B.did_r(mk, "rn32")
            B.did_w(mk, "rnm")
            mk = B.emit("q", B.dep_r("rnm") + B.dep_w("rnscr%d" % cpar),
                        lambda q, c_=cpar: q.dma_start(out=rnscr[c_], in_=rnm[:]), rsem)
            B.did_r(mk, "rnm"); B.did_w(mk, "rnscr%d" % cpar)
            for mt in range(NT):
                p = mt % 2
                mk = B.emit("g", B.dep_r("rnscr%d" % cpar) + B.dep_w("rnb%d" % p),
                            lambda g, m_=mt, p_=p, c_=cpar: g.dma_start(
                                out=rnb[p_][:], in_=rnscr[c_, m_].unsqueeze(0).broadcast_to((128, RK))), rsem)
                B.did_w(mk, "rnb%d" % p); B.did_r(mk, "rnscr%d" % cpar)
                mk = B.emit("v", B.dep_r("rnb%d" % p, srcname + "_t%d" % mt),
                            lambda v, m_=mt, p_=p, x_=srcbuf: v.tensor_tensor(
                                x_[:, m_ * RK:(m_ + 1) * RK], x_[:, m_ * RK:(m_ + 1) * RK],
                                rnb[p_][:], op=ALU.mult), "V")
                B.did_w(mk, srcname + "_t%d" % mt); B.did_r(mk, "rnb%d" % p)

        def k_chunk(c):
            own = c >= 6
            cpar = c % 2
            xdst = xn_own if own else xn_roll
            xname = "xn_own" if own else "xn_roll"
            dma_x_chunk(c, xdst, xname)
            dma_mask(c)
            ln_chunk(c, xdst, xname, own)
            proj(xdst, xname, wkv, kvb, "kvb", 0, 1)
            norms(kvb, "kvb", cpar, C_KV, "mkb%d" % cpar)
            cold, cnew = carry[c % 2], carry[1 - c % 2]
            coldn, cnewn = "carry%d" % (c % 2), "carry%d" % (1 - c % 2)
            for mt in range(NT):
                if own:
                    oc = c - 6
                    kname = "kvc%d" % oc
                    mk = B.emit("v", B.dep_r("kvb_t%d" % mt, coldn, "onesS") + B.dep_w(kname + "_t%d" % mt),
                                lambda v, m_=mt, d_=kvc[0], co_=cold: v.tensor_tensor_scan(
                                    d_[:, m_ * RK:(m_ + 1) * RK], onesS[:],
                                    kvb[:, m_ * RK:(m_ + 1) * RK], co_[:, m_:m_ + 1],
                                    op0=ALU.mult, op1=ALU.add), "V")
                    B.did_w(mk, kname + "_t%d" % mt)
                    B.did_r(mk, "kvb_t%d" % mt, coldn, "onesS")
                mk = B.emit("v", B.dep_r("kvb_t%d" % mt) + B.dep_w("ctmp"),
                            lambda v, m_=mt: v.tensor_reduce(
                                ctmp[:, m_:m_ + 1], kvb[:, m_ * RK:(m_ + 1) * RK],
                                axis=mybir.AxisListType.X, op=ALU.add), "V")
                B.did_w(mk, "ctmp"); B.did_r(mk, "kvb_t%d" % mt)
                mk = B.emit("v", B.dep_r("ctmp", coldn) + (B.dep_w(cnewn) if mt == 0 else []),
                            lambda v, m_=mt, co_=cold, cn_=cnew: v.tensor_tensor(
                                cn_[:, m_:m_ + 1], co_[:, m_:m_ + 1], ctmp[:, m_:m_ + 1], op=ALU.add), "V")
                B.did_r(mk, "ctmp", coldn)
                if mt == NT - 1:
                    B.did_w(mk, cnewn)

        def q_chunk(oc):
            cpar = oc
            kname = "kvc0"
            proj(xn_own, "xn_own", wq, qsb, "qsb", 0, 1)
            norms(qsb, "qsb", cpar, C_Q, None)
            for mt in range(NT):
                p = mt % 2
                names = [f"hb{p}_{i}" for i in range(6)]
                pa, pb = P[2 * p], P[2 * p + 1]
                pan, pbn = "P%d" % (2 * p), "P%d" % (2 * p + 1)
                for (src, srcn, mat, matn, dsti) in (
                        (qsb, "qsb_t%d" % mt, cfb, "cfb", 0),
                        (qsb, "qsb_t%d" % mt, sfb, "sfb", 1),
                        (kvc[0], kname + "_t%d" % mt, cfb, "cfb", 2),
                        (kvc[0], kname + "_t%d" % mt, sfb, "sfb", 3)):
                    pp, pn = (pa, pan) if dsti % 2 == 0 else (pb, pbn)
                    mk = B.emit("t", B.dep_r(srcn, matn) + B.dep_w(pn),
                                lambda t, s_=src, m_=mt, w_=mat, p_=pp: t.matmul(
                                    p_[:], w_[:], s_[:, m_ * RK:(m_ + 1) * RK], start=True, stop=True), "T")
                    B.did_w(mk, pn); B.did_r(mk, srcn, matn)
                    mk = B.emit("s", B.dep_r(pn) + B.dep_w(names[dsti]),
                                lambda s, b_=hb[p][dsti], p_=pp: s.copy(b_[:], p_[:]), "S")
                    B.did_r(mk, pn); B.did_w(mk, names[dsti])
                def prods(v, b_=hb[p]):
                    QC, QS, KC, KS, T5, T6 = b_
                    v.tensor_tensor(T5[:], QC[:], KC[:], op=ALU.mult)
                    v.tensor_tensor(T6[:], QS[:], KS[:], op=ALU.mult)
                    v.tensor_tensor(T5[:], T5[:], T6[:], op=ALU.subtract)
                    v.tensor_tensor(T6[:], QC[:], KS[:], op=ALU.mult)
                    v.tensor_tensor(QC[:], QS[:], KC[:], op=ALU.mult)
                    return v.tensor_tensor(T6[:], T6[:], QC[:], op=ALU.add)
                mk = B.emit("v", B.dep_r(*names) + B.dep_w(names[0], names[4], names[5]), prods, "V")
                B.did_r(mk, *names); B.did_w(mk, names[0], names[4], names[5])
                mk = B.emit("t", B.dep_r(names[4], names[5], "cfb", "sfb") + B.dep_w("P4"),
                            lambda t, b_=hb[p]: (t.matmul(P[4][:], cfb[:], b_[4][:], start=True, stop=False),
                                                 t.matmul(P[4][:], sfb[:], b_[5][:], start=False, stop=True))[-1], "T")
                B.did_w(mk, "P4"); B.did_r(mk, names[4], names[5], "cfb", "sfb")
                mk = B.emit("s", B.dep_r("P4") + B.dep_w("vh_t%d" % mt),
                            lambda s, m_=mt: s.copy(vh[:, m_ * RK:(m_ + 1) * RK], P[4][:]), "S")
                B.did_r(mk, "P4"); B.did_w(mk, "vh_t%d" % mt)
            vh_names = ["vh_t%d" % k for k in range(NT)]
            dma_wtile(wo, 0, wbuf, "wb", "DW")
            dma_wtile(wo, 1, wbuf, "wb", "DW")
            for mt in range(NT):
                par = mt % 2
                pp, pn = P[6 + par], "P%d" % (6 + par)
                def mmo(t, p_=pp, w_=wbuf[par]):
                    r = None
                    for kt in range(NT):
                        r = t.matmul(p_[:], w_[:, kt * 128:(kt + 1) * 128],
                                     vh[:, kt * RK:(kt + 1) * RK], start=(kt == 0), stop=(kt == NT - 1))
                    return r
                mk = B.emit("t", B.dep_r("wb%d" % par, *vh_names) + B.dep_w(pn), mmo, "T")
                B.did_w(mk, pn); B.did_r(mk, "wb%d" % par, *vh_names)
                if mt + 2 < NT:
                    dma_wtile(wo, mt + 2, wbuf, "wb", "DW")
                mk = B.emit("s", B.dep_r(pn) + B.dep_w("ast%d" % par),
                            lambda s, p_=pp, a_=attn_st[par]: s.copy(a_[:], p_[:]), "S")
                B.did_r(mk, pn); B.did_w(mk, "ast%d" % par)
                mk = B.emit("q", B.dep_r("ast%d" % par) + B.dep_w("ascr_t%d" % mt),
                            lambda q, m_=mt, a_=attn_st[par]: q.dma_start(
                                out=ascr[m_ * 128:(m_ + 1) * 128, :], in_=a_[:]), "DA%d" % par)
                B.did_r(mk, "ast%d" % par); B.did_w(mk, "ascr_t%d" % mt)

        def f_chunk(oc, sub):
            cs = sub * RF
            outcol = oc * RK + cs
            dma_wtile(wg, 0, wgb, "wgb", "DG")
            dma_wtile(wu, 0, wub, "wub", "DU")
            dma_wtile(wg, 1, wgb, "wgb", "DG")
            dma_wtile(wu, 1, wub, "wub", "DU")
            for mt in range(64):
                par = mt % 2
                pg, pgn = P[par], "P%d" % par
                pu, pun = P[2 + par], "P%d" % (2 + par)
                def mmg(t, p_=pg, w_=wgb[par]):
                    r = None
                    for kt in range(NT):
                        r = t.matmul(p_[:, 0:RF], w_[:, kt * 128:(kt + 1) * 128],
                                     xn_own[:, kt * RK + cs: kt * RK + cs + RF],
                                     start=(kt == 0), stop=(kt == NT - 1))
                    return r
                mk = B.emit("t", B.dep_r("wgb%d" % par, "xn_own") + B.dep_w(pgn), mmg, "T")
                B.did_w(mk, pgn); B.did_r(mk, "wgb%d" % par, "xn_own")
                if mt + 2 < 64:
                    dma_wtile(wg, mt + 2, wgb, "wgb", "DG")
                def mmu(t, p_=pu, w_=wub[par]):
                    r = None
                    for kt in range(NT):
                        r = t.matmul(p_[:, 0:RF], w_[:, kt * 128:(kt + 1) * 128],
                                     xn_own[:, kt * RK + cs: kt * RK + cs + RF],
                                     start=(kt == 0), stop=(kt == NT - 1))
                    return r
                mk = B.emit("t", B.dep_r("wub%d" % par, "xn_own") + B.dep_w(pun), mmu, "T")
                B.did_w(mk, pun); B.did_r(mk, "wub%d" % par, "xn_own")
                if mt + 2 < 64:
                    dma_wtile(wu, mt + 2, wub, "wub", "DU")
                mk = B.emit("s", B.dep_r(pgn, "sgv") + B.dep_w("silg%d" % par),
                            lambda s, m_=mt, p_=pg, g_=silg[par]: s.activation(
                                g_[:], p_[:, 0:RF], AF.Silu, scale=sgv[:, m_:m_ + 1]), "S")
                B.did_r(mk, pgn, "sgv"); B.did_w(mk, "silg%d" % par)
                mk = B.emit("v", B.dep_r(pun, "suv", "silg%d" % par) + B.dep_w("fuse_t%d" % mt),
                            lambda v, m_=mt, p_=pu, g_=silg[par]: v.scalar_tensor_tensor(
                                fuse[:, m_ * RF:(m_ + 1) * RF], p_[:, 0:RF], suv[:, m_:m_ + 1], g_[:],
                                op0=ALU.mult, op1=ALU.mult), "V")
                B.did_r(mk, pun, "suv", "silg%d" % par); B.did_w(mk, "fuse_t%d" % mt)
            dma_wd_piece(0, 0, 0)
            dma_wd_piece(1, 0, 1)
            fuse_names = ["fuse_t%d" % i for i in range(64)]
            for ot in range(NT):
                par = ot % 2
                pd, pdn = P[4 + par], "P%d" % (4 + par)
                mk = B.emit("q", B.dep_w("ard%d" % par) + B.dep_r("ascr_t%d" % ot),
                            lambda q, o_=ot, a_=a_rd[par]: q.dma_start(
                                out=a_[:], in_=ascr[o_ * 128:(o_ + 1) * 128, cs:cs + RF]), "DB%d" % par)
                B.did_w(mk, "ard%d" % par); B.did_r(mk, "ascr_t%d" % ot)
                mk = B.emit("g", B.dep_w("xrd%d" % par),
                            lambda g, o_=ot, x_=x_rd[par]: g.dma_start(
                                out=x_[:], in_=xkv[o_ * 128:(o_ + 1) * 128, 3072 + outcol: 3072 + outcol + RF]),
                            "DB%d" % par)
                B.did_w(mk, "xrd%d" % par)
                mk = B.emit("v", B.dep_r("ard%d" % par, "xrd%d" % par) + B.dep_w("axr%d" % par),
                            lambda v, a_=a_rd[par], x_=x_rd[par], r_=axr[par]: v.tensor_tensor(
                                r_[:], a_[:], x_[:], op=ALU.add), "V")
                B.did_r(mk, "ard%d" % par, "xrd%d" % par); B.did_w(mk, "axr%d" % par)
                for pc in range(4):
                    idx = ot * 4 + pc
                    dpar = idx % 2
                    deps = (B.dep_r("wdb%d" % dpar) + [B.dep_r(f)[0] for f in fuse_names[pc * 16:(pc + 1) * 16]] +
                            (B.dep_w(pdn) if pc == 0 else []))
                    def mmd(t, p_=pd, w_=wdb[dpar], pc_=pc):
                        r = None
                        for i in range(16):
                            kt = pc_ * 16 + i
                            r = t.matmul(p_[:, 0:RF], w_[:, i * 128:(i + 1) * 128],
                                         fuse[:, kt * RF:(kt + 1) * RF],
                                         start=(kt == 0), stop=(kt == 63))
                        return r
                    mk = B.emit("t", deps, mmd, "T")
                    B.did_r(mk, "wdb%d" % dpar, *fuse_names[pc * 16:(pc + 1) * 16])
                    if pc == 3:
                        B.did_w(mk, pdn)
                    if idx + 2 < 64:
                        nxt = idx + 2
                        dma_wd_piece(nxt, nxt // 4, nxt % 4)
                mk = B.emit("v", B.dep_r(pdn, "sdv", "axr%d" % par) + B.dep_w("ost%d" % par),
                            lambda v, o_=ot, p_=pd, r_=axr[par], u_=outst[par]: v.scalar_tensor_tensor(
                                u_[:], p_[:, 0:RF], sdv[:, o_:o_ + 1], r_[:], op0=ALU.mult, op1=ALU.add), "V")
                B.did_r(mk, pdn, "sdv", "axr%d" % par); B.did_w(mk, "ost%d" % par)
                mk = B.emit("q", B.dep_r("ost%d" % par),
                            lambda q, o_=ot, u_=outst[par]: q.dma_start(
                                out=out[o_ * 128:(o_ + 1) * 128, outcol:outcol + RF], in_=u_[:]), "DO")
                B.did_r(mk, "ost%d" % par)

        for c in range(6):
            k_chunk(c)
        k_chunk(6)
        q_chunk(0)
        f_chunk(0, 0)
        f_chunk(0, 1)
        k_chunk(7)
        q_chunk(1)
        f_chunk(1, 0)
        f_chunk(1, 1)
        B.emit("q", [("DO", B.cnt["DO"])], lambda q: q.nop(), "G")
        B.flush()
    return nc


_NC_CACHE = None


def kernel(L0_attn_x, L0_attn_W_q, L0_attn_W_kv, L0_attn_W_o,
           L0_ffn_w_gate, L0_ffn_w_up, L0_ffn_w_down,
           L0_ffn_s_gate, L0_ffn_s_up, L0_ffn_s_down):
    x = np.asarray(L0_attn_x, dtype=np.float32)
    Wq = np.asarray(L0_attn_W_q, dtype=np.float32)
    Wkv = np.asarray(L0_attn_W_kv, dtype=np.float32)
    Wo = np.asarray(L0_attn_W_o, dtype=np.float32)
    wg = np.asarray(L0_ffn_w_gate, dtype=np.int8)
    wu = np.asarray(L0_ffn_w_up, dtype=np.int8)
    wd = np.asarray(L0_ffn_w_down, dtype=np.int8)
    sg = np.asarray(L0_ffn_s_gate, dtype=np.float32)
    su = np.asarray(L0_ffn_s_up, dtype=np.float32)
    sdn = np.asarray(L0_ffn_s_down, dtype=np.float32)

    t4a = lambda W: np.ascontiguousarray(W.reshape(16, 128, 16, 128).transpose(0, 3, 2, 1))
    wq_t, wkv_t, wo_t = t4a(Wq), t4a(Wkv), t4a(Wo)
    wg_t = np.ascontiguousarray(wg.reshape(16, 128, 64, 128).transpose(2, 1, 0, 3))
    wu_t = np.ascontiguousarray(wu.reshape(16, 128, 64, 128).transpose(2, 1, 0, 3))
    wd_t = np.ascontiguousarray(wd.reshape(64, 128, 16, 128).transpose(2, 1, 0, 3))
    sg_t = np.ascontiguousarray(sg.reshape(64, 128).T)
    su_t = np.ascontiguousarray(su.reshape(64, 128).T)
    sd_t = np.ascontiguousarray(sdn.reshape(16, 128).T)
    n = np.arange(128)
    ang = 2.0 * np.pi * np.outer(n, n) / 128.0
    cf = np.cos(ang).astype(np.float32)
    sf = np.sin(ang).astype(np.float32)

    in_maps = []
    for c in range(8):
        b, j = c // 4, c % 4
        xb = x[b]
        rolled = np.concatenate([xb[0:1024]] * (3 - j) + [xb[0:(j + 1) * 1024]], axis=0)
        x_kv = np.ascontiguousarray(rolled.T)
        m = np.zeros((16, SEQ), dtype=np.float32)
        m[:, (3 - j) * 1024:] = 1.0
        in_maps.append({
            "xkv": x_kv, "wq": wq_t, "wkv": wkv_t, "wo": wo_t,
            "wg": wg_t, "wu": wu_t, "wd": wd_t,
            "sg": sg_t, "su": su_t, "sd": sd_t,
            "cf": cf, "sf": sf, "mask": m,
        })

    global _NC_CACHE
    if _NC_CACHE is None:
        _NC_CACHE = build_program()
    res = run_bass_kernel_spmd(_NC_CACHE, in_maps, core_ids=list(range(8)))
    outp = np.empty((2, SEQ, D), dtype=np.float32)
    for c in range(8):
        b, j = c // 4, c % 4
        outp[b, j * 1024:(j + 1) * 1024, :] = np.asarray(res.results[c]["out"]).T
    return outp


# version 03
# speedup vs baseline: 1.1544x; 1.0018x over previous
import contextlib
import numpy as np
import concourse.bass as bass
import concourse.mybir as mybir
from concourse.bass_utils import run_bass_kernel_spmd

F32 = mybir.dt.float32
BF16 = mybir.dt.bfloat16
I8 = mybir.dt.int8
AF = mybir.ActivationFunctionType
ALU = mybir.AluOpType

D = 2048
NT = 16
SEQ = 4096
RK = 512
NCH = SEQ // RK
RF = 256
C_Q = float(128 ** 3)
C_KV = 128.0

_EK = ("t", "v", "s", "g", "q")


class Bld:
    def __init__(self, nc, block, semnames, es):
        self.nc = nc
        self.sec = {"t": block.tensor, "v": block.vector, "s": block.scalar,
                    "g": block.gpsimd, "q": block.sync}
        self.sem, self.cnt = {}, {}
        self.seen = {k: {} for k in _EK}
        for n in semnames:
            self.sem[n] = es.enter_context(nc.semaphore(n))
            self.cnt[n] = 0
        self.wr, self.rd = {}, {}
        self.pend = {}

    def emit(self, ekey, deps, fn, inc):
        step = 16 if inc.startswith("D") else 1
        todo, seen = [], self.seen[ekey]
        for d in deps:
            if d is None:
                continue
            s, v = d
            if v > 0 and seen.get(s, 0) < v:
                todo.append((s, v))
                seen[s] = v
        semh, inc_sem = self.sem, self.sem[inc]

        def body(eng):
            for s, v in todo:
                eng.wait_ge(semh[s], v)
            fn(eng).then_inc(inc_sem, step)

        self.pend.setdefault(ekey, []).append(body)
        self.cnt[inc] += step
        return (inc, self.cnt[inc])

    def flush(self):
        for ekey, bodies in self.pend.items():
            def run_all(eng, bodies_=bodies):
                for b in bodies_:
                    b(eng)
            self.sec[ekey](run_all)
        self.pend = {}

    def dep_r(self, *bufs):
        return [self.wr.get(b) for b in bufs]

    def dep_w(self, *bufs):
        out = []
        for b in bufs:
            out.extend(self.rd.get(b, {}).items())
            if b in self.wr:
                out.append(self.wr[b])
        return out

    def did_r(self, mark, *bufs):
        for b in bufs:
            d = self.rd.setdefault(b, {})
            if d.get(mark[0], 0) < mark[1]:
                d[mark[0]] = mark[1]

    def did_w(self, mark, *bufs):
        for b in bufs:
            self.wr[b] = mark
            self.rd.pop(b, None)


def build_program():
    nc = bass.Bass("TRN2", target_bir_lowering=False)
    dti = lambda n, s, d: nc.dram_tensor(n, s, d, kind="ExternalInput").ap()
    xkv = dti("xkv", [D, SEQ], F32)
    wq = dti("wq", [16, 128, 16, 128], F32)
    wkv = dti("wkv", [16, 128, 16, 128], F32)
    wo = dti("wo", [16, 128, 16, 128], F32)
    wg = dti("wg", [64, 128, 16, 128], I8)
    wu = dti("wu", [64, 128, 16, 128], I8)
    wd = dti("wd", [16, 128, 64, 128], I8)
    sg = dti("sg", [128, 64], F32)
    su = dti("su", [128, 64], F32)
    sd = dti("sd", [128, 16], F32)
    cfm = dti("cf", [128, 128], F32)
    sfm = dti("sf", [128, 128], F32)
    mask = dti("mask", [16, SEQ], F32)
    out = nc.dram_tensor("out", [D, 1024], F32, kind="ExternalOutput").ap()
    nscr = nc.dram_tensor("nscr", [2, 16, RK], BF16, kind="Internal").ap()
    rnscr = nc.dram_tensor("rnscr", [2, 16, RK], BF16, kind="Internal").ap()
    ascr = nc.dram_tensor("ascr", [D, RK], BF16, kind="Internal").ap()

    sems = ["T", "V", "S", "G", "DC",
            "DX0", "DX1", "DW0", "DW1", "DG0", "DG1", "DU0", "DU1",
            "DD0", "DD1", "DN0", "DN1", "DR0", "DR1", "DA0", "DA1",
            "DB0", "DB1", "DM0", "DM1", "DO"]

    es = contextlib.ExitStack()
    with es:
        sb = lambda n, s, d: es.enter_context(nc.sbuf_tensor(n, s, d))
        xn_roll = sb("xn_roll", [128, NT * RK], BF16)
        xn_own = sb("xn_own", [128, NT * RK], BF16)
        kvb = sb("kvb", [128, NT * RK], BF16)
        carry = [sb("carry0", [128, NT], F32), sb("carry1", [128, NT], F32)]
        ctmp = sb("ctmp", [128, NT], F32)
        sqt = [sb("sqt0", [128, RK], BF16), sb("sqt1", [128, RK], BF16)]
        nbuf = [sb("nbuf0", [1, RK], BF16), sb("nbuf1", [1, RK], BF16)]
        n16 = sb("n16", [16, RK], BF16)
        rn32 = sb("rn32", [16, RK], F32)
        rnm = sb("rnm", [16, RK], BF16)
        rnb = [sb("rnb0", [128, RK], BF16), sb("rnb1", [128, RK], BF16)]
        mv16 = sb("mv16", [1, RK], BF16)
        e2f = sb("e2f", [1, RK], BF16)
        vtmp = sb("vtmp", [1, RK], BF16)
        rstdf = sb("rstdf", [1, RK], F32)
        rstd16 = sb("rstd16", [1, RK], BF16)
        qsb = sb("qsb", [128, NT * RK], BF16)
        vh = sb("vh", [128, NT * RK], BF16)
        hb = [[sb(f"hb{p}_{i}", [128, RK], BF16) for i in range(6)] for p in range(2)]
        attn_st = [sb("ast0", [128, RK], BF16)]
        wbuf = [sb("wb0", [128, 2048], BF16), sb("wb1", [128, 2048], BF16)]
        wgb = [sb("wgb0", [128, 2048], BF16), sb("wgb1", [128, 2048], BF16)]
        wub = [sb("wub0", [128, 2048], BF16), sb("wub1", [128, 2048], BF16)]
        wdb = [sb("wdb0", [128, 2048], BF16), sb("wdb1", [128, 2048], BF16)]
        fuse = sb("fuse", [128, 16 * RK], BF16)
        silg = [sb("silg0", [128, RK], BF16), sb("silg1", [128, RK], BF16)]
        sgv = sb("sgv", [128, 64], F32)
        suv = sb("suv", [128, 64], F32)
        sdv = sb("sdv", [128, 16], F32)
        cfb = sb("cfb", [128, 128], BF16)
        sfb = sb("sfb", [128, 128], BF16)
        onesM = sb("onesM", [128, 1], BF16)
        ones1 = sb("ones1", [1, 128], BF16)
        onesS = sb("onesS", [128, RK], BF16)
        maskb = [sb("mkb0", [16, RK], BF16), sb("mkb1", [16, RK], BF16)]
        a_rd = [sb("ard0", [128, RK], BF16), sb("ard1", [128, RK], BF16)]
        x_rd = [sb("xrd0", [128, RK], BF16), sb("xrd1", [128, RK], BF16)]
        axr = [sb("axr0", [128, RK], F32), sb("axr1", [128, RK], F32)]
        outst = [sb("ost0", [128, RK], F32), sb("ost1", [128, RK], F32)]
        acc = sb("acc", [128, NT * RK], F32)
        P = [es.enter_context(nc.psum_tensor(f"P{i}", [128, 512], F32)) for i in range(8)]
        sbuf_left = nc.sbuf_bytes_remaining
        print('SBUF remaining after alloc:', sbuf_left)
        block = es.enter_context(nc.Block())
        B = Bld(nc, block, sems, es)
        B.sbuf_left = sbuf_left

        def prolog(g):
            g.memset(onesM[:], 1.0)
            g.memset(ones1[:], 1.0)
            g.memset(onesS[:], 1.0)
            g.memset(carry[0][:], 0.0)
            return g.memset(carry[1][:], 0.0)
        mk = B.emit("g", [], prolog, "G")
        B.did_w(mk, "onesM", "ones1", "onesS", "carry0", "carry1")
        for dst, src, nm in ((sgv, sg, "sgv"), (suv, su, "suv"), (sdv, sd, "sdv"),
                             (cfb, cfm, "cfb"), (sfb, sfm, "sfb")):
            mk = B.emit("g", [], (lambda d_, s_: (lambda g: g.dma_start(out=d_[:], in_=s_)))(dst, src), "DC")
            B.did_w(mk, nm)

        def dma_x_chunk(c, dst, xname):
            sem = "DX%d" % (c % 2)
            mk = B.emit("g", B.dep_w(xname),
                        lambda g, d_=dst, c_=c: g.dma_start(
                            out=d_[:, 0:NT * RK].rearrange("p (t r) -> p t r", t=NT),
                            in_=xkv[:, c_ * RK:(c_ + 1) * RK].rearrange("(t p) r -> p t r", p=128)),
                        sem)
            B.did_w(mk, xname, *[xname + "_t%d" % i for i in range(NT)])
            return mk

        def dma_mask(c):
            p = c % 2
            mk = B.emit("g", B.dep_w("mkb%d" % p),
                        lambda g, c_=c, p_=p: g.dma_start(out=maskb[p_][:], in_=mask[:, c_ * RK:(c_ + 1) * RK]),
                        "DM%d" % p)
            B.did_w(mk, "mkb%d" % p)

        def dma_wtile(wsrc, mt, bufs, bname, sem_base):
            par = mt % 2
            nm = bname + str(par)
            mk = B.emit("g", B.dep_w(nm),
                        lambda g, w_=wsrc, m_=mt, d_=bufs[par]: g.dma_start(
                            out=d_[:], in_=w_[m_].rearrange("p a b -> p (a b)")),
                        "%s%d" % (sem_base, par))
            B.did_w(mk, nm)

        def dma_wd_piece(ot, grp, par):
            nm = "wdb%d" % par
            mk = B.emit("g", B.dep_w(nm),
                        lambda g, o_=ot, p_=grp, d_=wdb[par]: g.dma_start(
                            out=d_[:], in_=wd[o_, :, p_ * 16:(p_ + 1) * 16, :].rearrange("p a b -> p (a b)")),
                        "DD%d" % par)
            B.did_w(mk, nm)

        def ln_chunk(c, xdst, xname, own):
            def mm_sum(t, x_=xdst):
                r = None
                for kt in range(NT):
                    r = t.matmul(P[2][0:1, :], onesM[:], x_[:, kt * RK:(kt + 1) * RK],
                                 start=(kt == 0), stop=(kt == NT - 1))
                return r
            mk = B.emit("t", B.dep_r(xname, "onesM") + B.dep_w("P2"), mm_sum, "T")
            B.did_w(mk, "P2"); B.did_r(mk, xname, "onesM")
            mk = B.emit("v", B.dep_r("P2") + B.dep_w("mv16"),
                        lambda v: v.tensor_scalar_mul(mv16[0:1, :], P[2][0:1, :], 1.0 / D), "V")
            B.did_w(mk, "mv16"); B.did_r(mk, "P2")
            if own:
                for kt in range(NT):
                    p = kt % 2
                    mk = B.emit("v", B.dep_r(xname) + B.dep_w("sqt%d" % p),
                                lambda v, k_=kt, p_=p, x_=xdst: v.tensor_tensor(
                                    sqt[p_][:], x_[:, k_ * RK:(k_ + 1) * RK],
                                    x_[:, k_ * RK:(k_ + 1) * RK], op=ALU.mult), "V")
                    B.did_w(mk, "sqt%d" % p); B.did_r(mk, xname)
                    mk2 = B.emit("t", B.dep_r("sqt%d" % p, "onesM") + (B.dep_w("P3") if kt == 0 else []),
                                 lambda t, k_=kt, p_=p: t.matmul(P[3][0:1, :], onesM[:], sqt[p_][:],
                                                                 start=(k_ == 0), stop=(k_ == NT - 1)), "T")
                    B.did_r(mk2, "sqt%d" % p, "onesM")
                    if kt == NT - 1:
                        B.did_w(mk2, "P3")
                mk = B.emit("v", B.dep_r("P3") + B.dep_w("e2f"),
                            lambda v: v.tensor_scalar_mul(e2f[0:1, :], P[3][0:1, :], 1.0 / D), "V")
                B.did_w(mk, "e2f"); B.did_r(mk, "P3")
                mk = B.emit("v", B.dep_r("mv16") + B.dep_w("vtmp"),
                            lambda v: v.tensor_tensor(vtmp[0:1, :], mv16[0:1, :], mv16[0:1, :], op=ALU.mult), "V")
                B.did_w(mk, "vtmp"); B.did_r(mk, "mv16")
                mk = B.emit("v", B.dep_r("e2f") + B.dep_w("vtmp"),
                            lambda v: v.tensor_tensor(vtmp[0:1, :], e2f[0:1, :], vtmp[0:1, :], op=ALU.subtract), "V")
                B.did_w(mk, "vtmp"); B.did_r(mk, "e2f")
                mk = B.emit("v", B.dep_r("vtmp"),
                            lambda v: v.tensor_scalar_add(vtmp[0:1, :], vtmp[0:1, :], 1e-5), "V")
                B.did_w(mk, "vtmp")
                mk = B.emit("s", B.dep_r("vtmp") + B.dep_w("rstdf"),
                            lambda s: s.activation(rstdf[0:1, :], vtmp[0:1, :], AF.Sqrt), "S")
                B.did_w(mk, "rstdf"); B.did_r(mk, "vtmp")
                mk = B.emit("v", B.dep_r("rstdf"),
                            lambda v: v.reciprocal(rstdf[0:1, :], rstdf[0:1, :]), "V")
                B.did_w(mk, "rstdf")
                mk = B.emit("v", B.dep_r("rstdf") + B.dep_w("rstd16"),
                            lambda v: v.tensor_copy(rstd16[0:1, :], rstdf[0:1, :]), "V")
                B.did_w(mk, "rstd16"); B.did_r(mk, "rstdf")
            mk = B.emit("t", B.dep_r("mv16", "ones1") + B.dep_w("P4"),
                        lambda t: t.matmul(P[4][:], ones1[:], mv16[0:1, :], start=True, stop=True), "T")
            B.did_w(mk, "P4"); B.did_r(mk, "mv16", "ones1")
            if own:
                mk = B.emit("t", B.dep_r("rstd16", "ones1") + B.dep_w("P5"),
                            lambda t: t.matmul(P[5][:], ones1[:], rstd16[0:1, :], start=True, stop=True), "T")
                B.did_w(mk, "P5"); B.did_r(mk, "rstd16", "ones1")
            for kt in range(NT):
                mk = B.emit("v", B.dep_r(xname, "P4"),
                            lambda v, k_=kt, x_=xdst: v.tensor_tensor(
                                x_[:, k_ * RK:(k_ + 1) * RK], x_[:, k_ * RK:(k_ + 1) * RK],
                                P[4][:], op=ALU.subtract), "V")
                B.did_w(mk, xname); B.did_r(mk, "P4")
                if own:
                    mk = B.emit("v", B.dep_r("P5"),
                                lambda v, k_=kt, x_=xdst: v.tensor_tensor(
                                    x_[:, k_ * RK:(k_ + 1) * RK], x_[:, k_ * RK:(k_ + 1) * RK],
                                    P[5][:], op=ALU.mult), "V")
                    B.did_w(mk, xname); B.did_r(mk, "P5")
            for i in range(NT):
                B.did_w(B.wr[xname], xname + "_t%d" % i)

        def proj(xsrc, xname, wsrc, dst, dstname, psA, psB):
            dma_wtile(wsrc, 0, wbuf, "wb", "DW")
            dma_wtile(wsrc, 1, wbuf, "wb", "DW")
            pps = [psA, psB]
            for mt in range(NT):
                par = mt % 2
                pp = P[pps[par]]
                pn = "P%d" % pps[par]
                def mm(t, p_=pp, w_=wbuf[par], x_=xsrc):
                    r = None
                    for kt in range(NT):
                        r = t.matmul(p_[:], w_[:, kt * 128:(kt + 1) * 128],
                                     x_[:, kt * RK:(kt + 1) * RK],
                                     start=(kt == 0), stop=(kt == NT - 1))
                    return r
                mk = B.emit("t", B.dep_r("wb%d" % par, xname) + B.dep_w(pn), mm, "T")
                B.did_w(mk, pn); B.did_r(mk, "wb%d" % par, xname)
                if mt + 2 < NT:
                    dma_wtile(wsrc, mt + 2, wbuf, "wb", "DW")
                mk = B.emit("s", B.dep_r(pn) + B.dep_w(dstname + "_t%d" % mt),
                            lambda s, m_=mt, p_=pp, d_=dst: s.copy(d_[:, m_ * RK:(m_ + 1) * RK], p_[:]), "S")
                B.did_r(mk, pn); B.did_w(mk, dstname + "_t%d" % mt)

        def norms(srcbuf, srcname, cpar, sq_scale, maskname):
            nsem = "DN%d" % cpar
            rsem = "DR%d" % cpar
            for mt in range(NT):
                p = mt % 2
                mk = B.emit("v", B.dep_r(srcname + "_t%d" % mt) + B.dep_w("sqt%d" % p),
                            lambda v, m_=mt, p_=p, x_=srcbuf: v.tensor_tensor(
                                sqt[p_][:], x_[:, m_ * RK:(m_ + 1) * RK],
                                x_[:, m_ * RK:(m_ + 1) * RK], op=ALU.mult), "V")
                B.did_w(mk, "sqt%d" % p); B.did_r(mk, srcname + "_t%d" % mt)
                pn = "P%d" % (6 + p)
                mk = B.emit("t", B.dep_r("sqt%d" % p, "onesM") + B.dep_w(pn),
                            lambda t, p_=p: t.matmul(P[6 + p_][0:1, :], onesM[:], sqt[p_][:],
                                                     start=True, stop=True), "T")
                B.did_w(mk, pn); B.did_r(mk, "sqt%d" % p, "onesM")
                mk = B.emit("s", B.dep_r(pn) + B.dep_w("nbuf%d" % p),
                            lambda s, p_=p: s.copy(nbuf[p_][0:1, :], P[6 + p_][0:1, :]), "S")
                B.did_r(mk, pn); B.did_w(mk, "nbuf%d" % p)
                mk = B.emit("q", B.dep_r("nbuf%d" % p) + B.dep_w("nscr%d" % cpar),
                            lambda q, m_=mt, p_=p, c_=cpar: q.dma_start(
                                out=nscr[c_, m_].unsqueeze(0), in_=nbuf[p_][0:1, :]), nsem)
                B.did_r(mk, "nbuf%d" % p)
                if mt == NT - 1:
                    B.did_w(mk, "nscr%d" % cpar)
                else:
                    B.did_w(mk, "nscr_part%d" % cpar)
            mk = B.emit("q", [(nsem, B.cnt[nsem])] + B.dep_w("n16"),
                        lambda q, c_=cpar: q.dma_start(out=n16[:], in_=nscr[c_]), nsem)
            B.did_w(mk, "n16"); B.did_r(mk, "nscr%d" % cpar)
            mk = B.emit("s", B.dep_r("n16") + B.dep_w("rn32"),
                        lambda s: s.activation(rn32[:], n16[:], AF.Sqrt, scale=float(sq_scale)), "S")
            B.did_w(mk, "rn32"); B.did_r(mk, "n16")
            mk = B.emit("v", B.dep_r("rn32"),
                        lambda v: v.reciprocal(rn32[:], rn32[:]), "V")
            B.did_w(mk, "rn32")
            if maskname is not None:
                mi = int(maskname[-1])
                mk = B.emit("v", B.dep_r("rn32", maskname) + B.dep_w("rnm"),
                            lambda v, m_=mi: v.tensor_tensor(rnm[:], rn32[:], maskb[m_][:], op=ALU.mult), "V")
                B.did_r(mk, "rn32", maskname)
            else:
                mk = B.emit("v", B.dep_r("rn32") + B.dep_w("rnm"),
                            lambda v: v.tensor_copy(rnm[:], rn32[:]), "V")
                B.did_r(mk, "rn32")
            B.did_w(mk, "rnm")
            mk = B.emit("q", B.dep_r("rnm") + B.dep_w("rnscr%d" % cpar),
                        lambda q, c_=cpar: q.dma_start(out=rnscr[c_], in_=rnm[:]), rsem)
            B.did_r(mk, "rnm"); B.did_w(mk, "rnscr%d" % cpar)
            for mt in range(NT):
                p = mt % 2
                mk = B.emit("g", B.dep_r("rnscr%d" % cpar) + B.dep_w("rnb%d" % p),
                            lambda g, m_=mt, p_=p, c_=cpar: g.dma_start(
                                out=rnb[p_][:], in_=rnscr[c_, m_].unsqueeze(0).broadcast_to((128, RK))), rsem)
                B.did_w(mk, "rnb%d" % p); B.did_r(mk, "rnscr%d" % cpar)
                mk = B.emit("v", B.dep_r("rnb%d" % p, srcname + "_t%d" % mt),
                            lambda v, m_=mt, p_=p, x_=srcbuf: v.tensor_tensor(
                                x_[:, m_ * RK:(m_ + 1) * RK], x_[:, m_ * RK:(m_ + 1) * RK],
                                rnb[p_][:], op=ALU.mult), "V")
                B.did_w(mk, srcname + "_t%d" % mt); B.did_r(mk, "rnb%d" % p)

        def k_chunk(c):
            own = c >= 6
            cpar = c % 2
            xdst = xn_own if own else xn_roll
            xname = "xn_own" if own else "xn_roll"
            dma_x_chunk(c, xdst, xname)
            dma_mask(c)
            ln_chunk(c, xdst, xname, own)
            proj(xdst, xname, wkv, kvb, "kvb", 0, 1)
            norms(kvb, "kvb", cpar, C_KV, "mkb%d" % cpar)
            cold, cnew = carry[c % 2], carry[1 - c % 2]
            coldn, cnewn = "carry%d" % (c % 2), "carry%d" % (1 - c % 2)
            for mt in range(NT):
                mk = B.emit("v", B.dep_r("kvb_t%d" % mt) + B.dep_w("ctmp"),
                            lambda v, m_=mt: v.tensor_reduce(
                                ctmp[:, m_:m_ + 1], kvb[:, m_ * RK:(m_ + 1) * RK],
                                axis=mybir.AxisListType.X, op=ALU.add), "V")
                B.did_w(mk, "ctmp"); B.did_r(mk, "kvb_t%d" % mt)
                mk = B.emit("v", B.dep_r("ctmp", coldn) + (B.dep_w(cnewn) if mt == 0 else []),
                            lambda v, m_=mt, co_=cold, cn_=cnew: v.tensor_tensor(
                                cn_[:, m_:m_ + 1], co_[:, m_:m_ + 1], ctmp[:, m_:m_ + 1], op=ALU.add), "V")
                B.did_r(mk, "ctmp", coldn)
                if mt == NT - 1:
                    B.did_w(mk, cnewn)
                if own:
                    mk = B.emit("v", B.dep_r("kvb_t%d" % mt, coldn, "onesS"),
                                lambda v, m_=mt, co_=cold: v.tensor_tensor_scan(
                                    kvb[:, m_ * RK:(m_ + 1) * RK], onesS[:],
                                    kvb[:, m_ * RK:(m_ + 1) * RK], co_[:, m_:m_ + 1],
                                    op0=ALU.mult, op1=ALU.add), "V")
                    B.did_w(mk, "kvb_t%d" % mt)
                    B.did_r(mk, coldn, "onesS")

        def q_chunk(oc):
            cpar = oc
            kname = "kvb"
            proj(xn_own, "xn_own", wq, qsb, "qsb", 0, 1)
            norms(qsb, "qsb", cpar, C_Q, None)
            for mt in range(NT):
                p = mt % 2
                names = [f"hb{p}_{i}" for i in range(6)]
                pa, pb = P[2 * p], P[2 * p + 1]
                pan, pbn = "P%d" % (2 * p), "P%d" % (2 * p + 1)
                for (src, srcn, mat, matn, dsti) in (
                        (qsb, "qsb_t%d" % mt, cfb, "cfb", 0),
                        (qsb, "qsb_t%d" % mt, sfb, "sfb", 1),
                        (kvb, kname + "_t%d" % mt, cfb, "cfb", 2),
                        (kvb, kname + "_t%d" % mt, sfb, "sfb", 3)):
                    pp, pn = (pa, pan) if dsti % 2 == 0 else (pb, pbn)
                    mk = B.emit("t", B.dep_r(srcn, matn) + B.dep_w(pn),
                                lambda t, s_=src, m_=mt, w_=mat, p_=pp: t.matmul(
                                    p_[:], w_[:], s_[:, m_ * RK:(m_ + 1) * RK], start=True, stop=True), "T")
                    B.did_w(mk, pn); B.did_r(mk, srcn, matn)
                    mk = B.emit("s", B.dep_r(pn) + B.dep_w(names[dsti]),
                                lambda s, b_=hb[p][dsti], p_=pp: s.copy(b_[:], p_[:]), "S")
                    B.did_r(mk, pn); B.did_w(mk, names[dsti])
                def prods(v, b_=hb[p]):
                    QC, QS, KC, KS, T5, T6 = b_
                    v.tensor_tensor(T5[:], QC[:], KC[:], op=ALU.mult)
                    v.tensor_tensor(T6[:], QS[:], KS[:], op=ALU.mult)
                    v.tensor_tensor(T5[:], T5[:], T6[:], op=ALU.subtract)
                    v.tensor_tensor(T6[:], QC[:], KS[:], op=ALU.mult)
                    v.tensor_tensor(QC[:], QS[:], KC[:], op=ALU.mult)
                    return v.tensor_tensor(T6[:], T6[:], QC[:], op=ALU.add)
                mk = B.emit("v", B.dep_r(*names) + B.dep_w(names[0], names[4], names[5]), prods, "V")
                B.did_r(mk, *names); B.did_w(mk, names[0], names[4], names[5])
                mk = B.emit("t", B.dep_r(names[4], names[5], "cfb", "sfb") + B.dep_w("P4"),
                            lambda t, b_=hb[p]: (t.matmul(P[4][:], cfb[:], b_[4][:], start=True, stop=False),
                                                 t.matmul(P[4][:], sfb[:], b_[5][:], start=False, stop=True))[-1], "T")
                B.did_w(mk, "P4"); B.did_r(mk, names[4], names[5], "cfb", "sfb")
                mk = B.emit("s", B.dep_r("P4") + B.dep_w("vh_t%d" % mt),
                            lambda s, m_=mt: s.copy(vh[:, m_ * RK:(m_ + 1) * RK], P[4][:]), "S")
                B.did_r(mk, "P4"); B.did_w(mk, "vh_t%d" % mt)
            vh_names = ["vh_t%d" % k for k in range(NT)]
            dma_wtile(wo, 0, wbuf, "wb", "DW")
            dma_wtile(wo, 1, wbuf, "wb", "DW")
            for mt in range(NT):
                par = mt % 2
                pp, pn = P[6 + par], "P%d" % (6 + par)
                def mmo(t, p_=pp, w_=wbuf[par]):
                    r = None
                    for kt in range(NT):
                        r = t.matmul(p_[:], w_[:, kt * 128:(kt + 1) * 128],
                                     vh[:, kt * RK:(kt + 1) * RK], start=(kt == 0), stop=(kt == NT - 1))
                    return r
                mk = B.emit("t", B.dep_r("wb%d" % par, *vh_names) + B.dep_w(pn), mmo, "T")
                B.did_w(mk, pn); B.did_r(mk, "wb%d" % par, *vh_names)
                if mt + 2 < NT:
                    dma_wtile(wo, mt + 2, wbuf, "wb", "DW")
                mk = B.emit("s", B.dep_r(pn) + B.dep_w("ast0"),
                            lambda s, p_=pp, a_=attn_st[0]: s.copy(a_[:], p_[:]), "S")
                B.did_r(mk, pn); B.did_w(mk, "ast0")
                mk = B.emit("q", B.dep_r("ast0") + B.dep_w("ascr_t%d" % mt),
                            lambda q, m_=mt, a_=attn_st[0]: q.dma_start(
                                out=ascr[m_ * 128:(m_ + 1) * 128, :], in_=a_[:]), "DA%d" % par)
                B.did_r(mk, "ast0"); B.did_w(mk, "ascr_t%d" % mt)

        def f_chunk(oc):
            outcol = oc * RK
            dma_wtile(wg, 0, wgb, "wgb", "DG")
            dma_wtile(wu, 0, wub, "wub", "DU")
            dma_wtile(wg, 1, wgb, "wgb", "DG")
            dma_wtile(wu, 1, wub, "wub", "DU")
            for grp in range(4):
                for i in range(16):
                    mt = grp * 16 + i
                    par = mt % 2
                    pg, pgn = P[par], "P%d" % par
                    pu, pun = P[2 + par], "P%d" % (2 + par)
                    def mmg(t, p_=pg, w_=wgb[par]):
                        r = None
                        for kt in range(NT):
                            r = t.matmul(p_[:], w_[:, kt * 128:(kt + 1) * 128],
                                         xn_own[:, kt * RK:(kt + 1) * RK],
                                         start=(kt == 0), stop=(kt == NT - 1))
                        return r
                    mk = B.emit("t", B.dep_r("wgb%d" % par, "xn_own") + B.dep_w(pgn), mmg, "T")
                    B.did_w(mk, pgn); B.did_r(mk, "wgb%d" % par, "xn_own")
                    if mt + 2 < 64:
                        dma_wtile(wg, mt + 2, wgb, "wgb", "DG")
                    def mmu(t, p_=pu, w_=wub[par]):
                        r = None
                        for kt in range(NT):
                            r = t.matmul(p_[:], w_[:, kt * 128:(kt + 1) * 128],
                                         xn_own[:, kt * RK:(kt + 1) * RK],
                                         start=(kt == 0), stop=(kt == NT - 1))
                        return r
                    mk = B.emit("t", B.dep_r("wub%d" % par, "xn_own") + B.dep_w(pun), mmu, "T")
                    B.did_w(mk, pun); B.did_r(mk, "wub%d" % par, "xn_own")
                    if mt + 2 < 64:
                        dma_wtile(wu, mt + 2, wub, "wub", "DU")
                    mk = B.emit("s", B.dep_r(pgn, "sgv") + B.dep_w("silg%d" % par),
                                lambda s, m_=mt, p_=pg, g_=silg[par]: s.activation(
                                    g_[:], p_[:], AF.Silu, scale=sgv[:, m_:m_ + 1]), "S")
                    B.did_r(mk, pgn, "sgv"); B.did_w(mk, "silg%d" % par)
                    mk = B.emit("v", B.dep_r(pun, "suv", "silg%d" % par) + B.dep_w("fuse_t%d" % i),
                                lambda v, m_=mt, i_=i, p_=pu, g_=silg[par]: v.scalar_tensor_tensor(
                                    fuse[:, i_ * RK:(i_ + 1) * RK], p_[:], suv[:, m_:m_ + 1], g_[:],
                                    op0=ALU.mult, op1=ALU.mult), "V")
                    B.did_r(mk, pun, "suv", "silg%d" % par); B.did_w(mk, "fuse_t%d" % i)
                if grp == 0:
                    dma_wd_piece(0, grp, 0)
                    dma_wd_piece(1, grp, 1)
                for ot in range(NT):
                    dpar = ot % 2
                    pd, pdn = P[4 + dpar], "P%d" % (4 + dpar)
                    fdeps = [B.dep_r("fuse_t%d" % i)[0] for i in range(16)]
                    deps = B.dep_r("wdb%d" % dpar) + fdeps + B.dep_w(pdn)
                    def mmd(t, p_=pd, w_=wdb[dpar]):
                        r = None
                        for i in range(16):
                            r = t.matmul(p_[:], w_[:, i * 128:(i + 1) * 128],
                                         fuse[:, i * RK:(i + 1) * RK],
                                         start=(i == 0), stop=(i == 15))
                        return r
                    mk = B.emit("t", deps, mmd, "T")
                    B.did_w(mk, pdn)
                    B.did_r(mk, "wdb%d" % dpar, *["fuse_t%d" % i for i in range(16)])
                    nidx = grp * 16 + ot + 2
                    if nidx < 64:
                        dma_wd_piece(nidx, nidx // 16, (nidx % 16) and None or None) if False else None
                    if ot + 2 < NT:
                        dma_wd_piece(ot + 2, grp, dpar)
                    elif grp + 1 < 4:
                        dma_wd_piece(ot + 2 - NT, grp + 1, dpar)
                    if grp == 0:
                        mk = B.emit("s", B.dep_r(pdn) + B.dep_w("acc_t%d" % ot),
                                    lambda s, o_=ot, p_=pd: s.copy(acc[:, o_ * RK:(o_ + 1) * RK], p_[:]), "S")
                    else:
                        mk = B.emit("v", B.dep_r(pdn, "acc_t%d" % ot),
                                    lambda v, o_=ot, p_=pd: v.tensor_tensor(
                                        acc[:, o_ * RK:(o_ + 1) * RK], acc[:, o_ * RK:(o_ + 1) * RK],
                                        p_[:], op=ALU.add), "V")
                    B.did_r(mk, pdn); B.did_w(mk, "acc_t%d" % ot)
            for ot in range(NT):
                par = ot % 2
                mk = B.emit("q", B.dep_w("ard%d" % par) + B.dep_r("ascr_t%d" % ot),
                            lambda q, o_=ot, a_=a_rd[par]: q.dma_start(
                                out=a_[:], in_=ascr[o_ * 128:(o_ + 1) * 128, :]), "DB%d" % par)
                B.did_w(mk, "ard%d" % par); B.did_r(mk, "ascr_t%d" % ot)
                mk = B.emit("g", B.dep_w("xrd%d" % par),
                            lambda g, o_=ot, x_=x_rd[par]: g.dma_start(
                                out=x_[:], in_=xkv[o_ * 128:(o_ + 1) * 128, 3072 + outcol: 3072 + outcol + RK]),
                            "DB%d" % par)
                B.did_w(mk, "xrd%d" % par)
                mk = B.emit("v", B.dep_r("ard%d" % par, "xrd%d" % par) + B.dep_w("axr%d" % par),
                            lambda v, a_=a_rd[par], x_=x_rd[par], r_=axr[par]: v.tensor_tensor(
                                r_[:], a_[:], x_[:], op=ALU.add), "V")
                B.did_r(mk, "ard%d" % par, "xrd%d" % par); B.did_w(mk, "axr%d" % par)
                mk = B.emit("v", B.dep_r("acc_t%d" % ot, "sdv", "axr%d" % par) + B.dep_w("ost%d" % par),
                            lambda v, o_=ot, r_=axr[par], u_=outst[par]: v.scalar_tensor_tensor(
                                u_[:], acc[:, o_ * RK:(o_ + 1) * RK], sdv[:, o_:o_ + 1], r_[:],
                                op0=ALU.mult, op1=ALU.add), "V")
                B.did_r(mk, "acc_t%d" % ot, "sdv", "axr%d" % par); B.did_w(mk, "ost%d" % par)
                mk = B.emit("q", B.dep_r("ost%d" % par),
                            lambda q, o_=ot, u_=outst[par]: q.dma_start(
                                out=out[o_ * 128:(o_ + 1) * 128, outcol:outcol + RK], in_=u_[:]), "DO")
                B.did_r(mk, "ost%d" % par)

        for c in range(6):
            k_chunk(c)
        k_chunk(6)
        q_chunk(0)
        f_chunk(0)
        k_chunk(7)
        q_chunk(1)
        f_chunk(1)
        B.emit("q", [("DO", B.cnt["DO"])], lambda q: q.nop(), "G")
        B.flush()
    return nc


_NC_CACHE = None


def kernel(L0_attn_x, L0_attn_W_q, L0_attn_W_kv, L0_attn_W_o,
           L0_ffn_w_gate, L0_ffn_w_up, L0_ffn_w_down,
           L0_ffn_s_gate, L0_ffn_s_up, L0_ffn_s_down):
    x = np.asarray(L0_attn_x, dtype=np.float32)
    Wq = np.asarray(L0_attn_W_q, dtype=np.float32)
    Wkv = np.asarray(L0_attn_W_kv, dtype=np.float32)
    Wo = np.asarray(L0_attn_W_o, dtype=np.float32)
    wg = np.asarray(L0_ffn_w_gate, dtype=np.int8)
    wu = np.asarray(L0_ffn_w_up, dtype=np.int8)
    wd = np.asarray(L0_ffn_w_down, dtype=np.int8)
    sg = np.asarray(L0_ffn_s_gate, dtype=np.float32)
    su = np.asarray(L0_ffn_s_up, dtype=np.float32)
    sdn = np.asarray(L0_ffn_s_down, dtype=np.float32)

    t4a = lambda W: np.ascontiguousarray(W.reshape(16, 128, 16, 128).transpose(0, 3, 2, 1))
    wq_t, wkv_t, wo_t = t4a(Wq), t4a(Wkv), t4a(Wo)
    wg_t = np.ascontiguousarray(wg.reshape(16, 128, 64, 128).transpose(2, 1, 0, 3))
    wu_t = np.ascontiguousarray(wu.reshape(16, 128, 64, 128).transpose(2, 1, 0, 3))
    wd_t = np.ascontiguousarray(wd.reshape(64, 128, 16, 128).transpose(2, 1, 0, 3))
    sg_t = np.ascontiguousarray(sg.reshape(64, 128).T)
    su_t = np.ascontiguousarray(su.reshape(64, 128).T)
    sd_t = np.ascontiguousarray(sdn.reshape(16, 128).T)
    n = np.arange(128)
    ang = 2.0 * np.pi * np.outer(n, n) / 128.0
    cf = np.cos(ang).astype(np.float32)
    sf = np.sin(ang).astype(np.float32)

    in_maps = []
    for c in range(8):
        b, j = c // 4, c % 4
        xb = x[b]
        rolled = np.concatenate([xb[0:1024]] * (3 - j) + [xb[0:(j + 1) * 1024]], axis=0)
        x_kv = np.ascontiguousarray(rolled.T)
        m = np.zeros((16, SEQ), dtype=np.float32)
        m[:, (3 - j) * 1024:] = 1.0
        in_maps.append({
            "xkv": x_kv, "wq": wq_t, "wkv": wkv_t, "wo": wo_t,
            "wg": wg_t, "wu": wu_t, "wd": wd_t,
            "sg": sg_t, "su": su_t, "sd": sd_t,
            "cf": cf, "sf": sf, "mask": m,
        })

    global _NC_CACHE
    if _NC_CACHE is None:
        _NC_CACHE = build_program()
    res = run_bass_kernel_spmd(_NC_CACHE, in_maps, core_ids=list(range(8)))
    outp = np.empty((2, SEQ, D), dtype=np.float32)
    for c in range(8):
        b, j = c // 4, c % 4
        outp[b, j * 1024:(j + 1) * 1024, :] = np.asarray(res.results[c]["out"]).T
    return outp


# version 04
# speedup vs baseline: 1.1870x; 1.0283x over previous
import contextlib
import numpy as np
import concourse.bass as bass
import concourse.mybir as mybir
from concourse.bass_utils import run_bass_kernel_spmd

F32 = mybir.dt.float32
BF16 = mybir.dt.bfloat16
I8 = mybir.dt.int8
AF = mybir.ActivationFunctionType
ALU = mybir.AluOpType

D = 2048
NT = 16
SEQ = 4096
RK = 512
NCH = SEQ // RK
RF = 256
C_Q = float(128 ** 3)
C_KV = 128.0

_EK = ("t", "v", "s", "g", "q")


class Bld:
    def __init__(self, nc, block, semnames, es):
        self.nc = nc
        self.sec = {"t": block.tensor, "v": block.vector, "s": block.scalar,
                    "g": block.gpsimd, "q": block.sync}
        self.sem, self.cnt = {}, {}
        self.seen = {k: {} for k in _EK}
        for n in semnames:
            self.sem[n] = es.enter_context(nc.semaphore(n))
            self.cnt[n] = 0
        self.wr, self.rd = {}, {}
        self.pend = {}

    def emit(self, ekey, deps, fn, inc):
        step = 16 if inc.startswith("D") else 1
        todo, seen = [], self.seen[ekey]
        for d in deps:
            if d is None:
                continue
            s, v = d
            if v > 0 and seen.get(s, 0) < v:
                todo.append((s, v))
                seen[s] = v
        semh, inc_sem = self.sem, self.sem[inc]

        def body(eng):
            for s, v in todo:
                eng.wait_ge(semh[s], v)
            fn(eng).then_inc(inc_sem, step)

        self.pend.setdefault(ekey, []).append(body)
        self.cnt[inc] += step
        return (inc, self.cnt[inc])

    def flush(self):
        for ekey, bodies in self.pend.items():
            def run_all(eng, bodies_=bodies):
                for b in bodies_:
                    b(eng)
            self.sec[ekey](run_all)
        self.pend = {}

    def dep_r(self, *bufs):
        return [self.wr.get(b) for b in bufs]

    def dep_w(self, *bufs):
        out = []
        for b in bufs:
            out.extend(self.rd.get(b, {}).items())
            if b in self.wr:
                out.append(self.wr[b])
        return out

    def did_r(self, mark, *bufs):
        for b in bufs:
            d = self.rd.setdefault(b, {})
            if d.get(mark[0], 0) < mark[1]:
                d[mark[0]] = mark[1]

    def did_w(self, mark, *bufs):
        for b in bufs:
            self.wr[b] = mark
            self.rd.pop(b, None)


def build_program():
    nc = bass.Bass("TRN2", target_bir_lowering=False)
    dti = lambda n, s, d: nc.dram_tensor(n, s, d, kind="ExternalInput").ap()
    xkv = dti("xkv", [D, SEQ], F32)
    wq = dti("wq", [16, 128, 16, 128], F32)
    wkv = dti("wkv", [16, 128, 16, 128], F32)
    wo = dti("wo", [16, 128, 16, 128], F32)
    wg = dti("wg", [64, 128, 16, 128], I8)
    wu = dti("wu", [64, 128, 16, 128], I8)
    wd = dti("wd", [16, 128, 64, 128], I8)
    sg = dti("sg", [128, 64], F32)
    su = dti("su", [128, 64], F32)
    sd = dti("sd", [128, 16], F32)
    cfm = dti("cf", [128, 128], F32)
    sfm = dti("sf", [128, 128], F32)
    mask = dti("mask", [16, SEQ], F32)
    out = nc.dram_tensor("out", [D, 1024], F32, kind="ExternalOutput").ap()
    nscr = nc.dram_tensor("nscr", [2, 16, RK], BF16, kind="Internal").ap()
    rnscr = nc.dram_tensor("rnscr", [2, 16, RK], BF16, kind="Internal").ap()
    ascr = nc.dram_tensor("ascr", [D, RK], BF16, kind="Internal").ap()

    sems = ["T", "V", "S", "G", "DC",
            "DX0", "DX1", "DW0", "DW1", "DG0", "DG1", "DU0", "DU1",
            "DD0", "DD1", "DN0", "DN1", "DR0", "DR1", "DA0", "DA1",
            "DB0", "DB1", "DM0", "DM1", "DO"]

    es = contextlib.ExitStack()
    with es:
        sb = lambda n, s, d: es.enter_context(nc.sbuf_tensor(n, s, d))
        xn_roll = sb("xn_roll", [128, NT * RK], BF16)
        xn_own = sb("xn_own", [128, NT * RK], BF16)
        kvb = sb("kvb", [128, NT * RK], BF16)
        carry = [sb("carry0", [128, NT], F32), sb("carry1", [128, NT], F32)]
        ctmp = sb("ctmp", [128, NT], F32)
        sqt = [sb("sqt0", [128, RK], BF16), sb("sqt1", [128, RK], BF16)]
        nbuf = [sb("nbuf0", [1, RK], BF16), sb("nbuf1", [1, RK], BF16)]
        n16 = sb("n16", [16, RK], BF16)
        rn32 = sb("rn32", [16, RK], F32)
        rnm = sb("rnm", [16, RK], BF16)
        rnb = [sb("rnb0", [128, RK], BF16), sb("rnb1", [128, RK], BF16)]
        mv16 = sb("mv16", [1, RK], BF16)
        e2f = sb("e2f", [1, RK], BF16)
        vtmp = sb("vtmp", [1, RK], BF16)
        rstdf = sb("rstdf", [1, RK], F32)
        rstd16 = sb("rstd16", [1, RK], BF16)
        qsb = sb("qsb", [128, NT * RK], BF16)
        vh = sb("vh", [128, NT * RK], BF16)
        hb = [[sb(f"hb{p}_{i}", [128, RK], BF16) for i in range(6)] for p in range(2)]
        attn_st = [sb("ast0", [128, RK], BF16)]
        wbuf = [sb("wb0", [128, 2048], BF16), sb("wb1", [128, 2048], BF16)]
        wgb = [sb("wgb0", [128, 2048], BF16), sb("wgb1", [128, 2048], BF16)]
        wub = [sb("wub0", [128, 2048], BF16), sb("wub1", [128, 2048], BF16)]
        wdb = [sb("wdb0", [128, 2048], BF16), sb("wdb1", [128, 2048], BF16)]
        fuse = sb("fuse", [128, 16 * RK], BF16)
        silg = [sb("silg0", [128, RK], BF16), sb("silg1", [128, RK], BF16)]
        sgv = sb("sgv", [128, 64], F32)
        suv = sb("suv", [128, 64], F32)
        sdv = sb("sdv", [128, 16], F32)
        cfb = sb("cfb", [128, 128], BF16)
        sfb = sb("sfb", [128, 128], BF16)
        onesM = sb("onesM", [128, 1], BF16)
        ones1 = sb("ones1", [1, 128], BF16)
        onesS = sb("onesS", [128, RK], BF16)
        maskb = [sb("mkb0", [16, RK], BF16), sb("mkb1", [16, RK], BF16)]
        a_rd = [sb("ard0", [128, RK], BF16), sb("ard1", [128, RK], BF16)]
        x_rd = [sb("xrd0", [128, RK], BF16), sb("xrd1", [128, RK], BF16)]
        axr = [sb("axr0", [128, RK], F32), sb("axr1", [128, RK], F32)]
        outst = [sb("ost0", [128, RK], F32), sb("ost1", [128, RK], F32)]
        acc = sb("acc", [128, NT * RK], F32)
        P = [es.enter_context(nc.psum_tensor(f"P{i}", [128, 512], F32)) for i in range(8)]
        sbuf_left = nc.sbuf_bytes_remaining
        print('SBUF remaining after alloc:', sbuf_left)
        block = es.enter_context(nc.Block())
        B = Bld(nc, block, sems, es)
        B.sbuf_left = sbuf_left

        def prolog(g):
            g.memset(onesM[:], 1.0)
            g.memset(ones1[:], 1.0)
            g.memset(onesS[:], 1.0)
            g.memset(carry[0][:], 0.0)
            return g.memset(carry[1][:], 0.0)
        mk = B.emit("g", [], prolog, "G")
        B.did_w(mk, "onesM", "ones1", "onesS", "carry0", "carry1")
        for dst, src, nm in ((sgv, sg, "sgv"), (suv, su, "suv"), (sdv, sd, "sdv"),
                             (cfb, cfm, "cfb"), (sfb, sfm, "sfb")):
            mk = B.emit("g", [], (lambda d_, s_: (lambda g: g.dma_start(out=d_[:], in_=s_)))(dst, src), "DC")
            B.did_w(mk, nm)

        def dma_x_chunk(c, dst, xname):
            sem = "DX%d" % (c % 2)
            mk = B.emit("g", B.dep_w(xname),
                        lambda g, d_=dst, c_=c: g.dma_start(
                            out=d_[:, 0:NT * RK].rearrange("p (t r) -> p t r", t=NT),
                            in_=xkv[:, c_ * RK:(c_ + 1) * RK].rearrange("(t p) r -> p t r", p=128)),
                        sem)
            B.did_w(mk, xname, *[xname + "_t%d" % i for i in range(NT)])
            return mk

        def dma_mask(c):
            p = c % 2
            mk = B.emit("g", B.dep_w("mkb%d" % p),
                        lambda g, c_=c, p_=p: g.dma_start(out=maskb[p_][:], in_=mask[:, c_ * RK:(c_ + 1) * RK]),
                        "DM%d" % p)
            B.did_w(mk, "mkb%d" % p)

        def dma_wtile(wsrc, mt, bufs, bname, sem_base):
            par = mt % 2
            nm = bname + str(par)
            mk = B.emit("g", B.dep_w(nm),
                        lambda g, w_=wsrc, m_=mt, d_=bufs[par]: g.dma_start(
                            out=d_[:], in_=w_[m_].rearrange("p a b -> p (a b)")),
                        "%s%d" % (sem_base, par))
            B.did_w(mk, nm)

        def dma_wd_piece(ot, grp, par):
            nm = "wdb%d" % par
            mk = B.emit("g", B.dep_w(nm),
                        lambda g, o_=ot, p_=grp, d_=wdb[par]: g.dma_start(
                            out=d_[:], in_=wd[o_, :, p_ * 16:(p_ + 1) * 16, :].rearrange("p a b -> p (a b)")),
                        "DD%d" % par)
            B.did_w(mk, nm)

        def ln_chunk(c, xdst, xname, own):
            def mm_sum(t, x_=xdst):
                r = None
                for kt in range(NT):
                    r = t.matmul(P[2][0:1, :], onesM[:], x_[:, kt * RK:(kt + 1) * RK],
                                 start=(kt == 0), stop=(kt == NT - 1))
                return r
            mk = B.emit("t", B.dep_r(xname, "onesM") + B.dep_w("P2"), mm_sum, "T")
            B.did_w(mk, "P2"); B.did_r(mk, xname, "onesM")
            mk = B.emit("v", B.dep_r("P2") + B.dep_w("mv16"),
                        lambda v: v.tensor_scalar_mul(mv16[0:1, :], P[2][0:1, :], 1.0 / D), "V")
            B.did_w(mk, "mv16"); B.did_r(mk, "P2")
            if own:
                for kt in range(NT):
                    p = kt % 2
                    mk = B.emit("v", B.dep_r(xname) + B.dep_w("sqt%d" % p),
                                lambda v, k_=kt, p_=p, x_=xdst: v.tensor_tensor(
                                    sqt[p_][:], x_[:, k_ * RK:(k_ + 1) * RK],
                                    x_[:, k_ * RK:(k_ + 1) * RK], op=ALU.mult), "V")
                    B.did_w(mk, "sqt%d" % p); B.did_r(mk, xname)
                    mk2 = B.emit("t", B.dep_r("sqt%d" % p, "onesM") + (B.dep_w("P3") if kt == 0 else []),
                                 lambda t, k_=kt, p_=p: t.matmul(P[3][0:1, :], onesM[:], sqt[p_][:],
                                                                 start=(k_ == 0), stop=(k_ == NT - 1)), "T")
                    B.did_r(mk2, "sqt%d" % p, "onesM")
                    if kt == NT - 1:
                        B.did_w(mk2, "P3")
                mk = B.emit("v", B.dep_r("P3") + B.dep_w("e2f"),
                            lambda v: v.tensor_scalar_mul(e2f[0:1, :], P[3][0:1, :], 1.0 / D), "V")
                B.did_w(mk, "e2f"); B.did_r(mk, "P3")
                mk = B.emit("v", B.dep_r("mv16") + B.dep_w("vtmp"),
                            lambda v: v.tensor_tensor(vtmp[0:1, :], mv16[0:1, :], mv16[0:1, :], op=ALU.mult), "V")
                B.did_w(mk, "vtmp"); B.did_r(mk, "mv16")
                mk = B.emit("v", B.dep_r("e2f") + B.dep_w("vtmp"),
                            lambda v: v.tensor_tensor(vtmp[0:1, :], e2f[0:1, :], vtmp[0:1, :], op=ALU.subtract), "V")
                B.did_w(mk, "vtmp"); B.did_r(mk, "e2f")
                mk = B.emit("v", B.dep_r("vtmp"),
                            lambda v: v.tensor_scalar_add(vtmp[0:1, :], vtmp[0:1, :], 1e-5), "V")
                B.did_w(mk, "vtmp")
                mk = B.emit("s", B.dep_r("vtmp") + B.dep_w("rstdf"),
                            lambda s: s.activation(rstdf[0:1, :], vtmp[0:1, :], AF.Sqrt), "S")
                B.did_w(mk, "rstdf"); B.did_r(mk, "vtmp")
                mk = B.emit("v", B.dep_r("rstdf"),
                            lambda v: v.reciprocal(rstdf[0:1, :], rstdf[0:1, :]), "V")
                B.did_w(mk, "rstdf")
                mk = B.emit("v", B.dep_r("rstdf") + B.dep_w("rstd16"),
                            lambda v: v.tensor_copy(rstd16[0:1, :], rstdf[0:1, :]), "V")
                B.did_w(mk, "rstd16"); B.did_r(mk, "rstdf")
            mk = B.emit("t", B.dep_r("mv16", "ones1") + B.dep_w("P4"),
                        lambda t: t.matmul(P[4][:], ones1[:], mv16[0:1, :], start=True, stop=True), "T")
            B.did_w(mk, "P4"); B.did_r(mk, "mv16", "ones1")
            if own:
                mk = B.emit("t", B.dep_r("rstd16", "ones1") + B.dep_w("P5"),
                            lambda t: t.matmul(P[5][:], ones1[:], rstd16[0:1, :], start=True, stop=True), "T")
                B.did_w(mk, "P5"); B.did_r(mk, "rstd16", "ones1")
            for kt in range(NT):
                mk = B.emit("v", B.dep_r(xname, "P4"),
                            lambda v, k_=kt, x_=xdst: v.tensor_tensor(
                                x_[:, k_ * RK:(k_ + 1) * RK], x_[:, k_ * RK:(k_ + 1) * RK],
                                P[4][:], op=ALU.subtract), "V")
                B.did_w(mk, xname); B.did_r(mk, "P4")
                if own:
                    mk = B.emit("v", B.dep_r("P5"),
                                lambda v, k_=kt, x_=xdst: v.tensor_tensor(
                                    x_[:, k_ * RK:(k_ + 1) * RK], x_[:, k_ * RK:(k_ + 1) * RK],
                                    P[5][:], op=ALU.mult), "V")
                    B.did_w(mk, xname); B.did_r(mk, "P5")
            for i in range(NT):
                B.did_w(B.wr[xname], xname + "_t%d" % i)

        def proj(xsrc, xname, wsrc, dst, dstname, psA, psB):
            dma_wtile(wsrc, 0, wbuf, "wb", "DW")
            dma_wtile(wsrc, 1, wbuf, "wb", "DW")
            pps = [psA, psB]
            for mt in range(NT):
                par = mt % 2
                pp = P[pps[par]]
                pn = "P%d" % pps[par]
                def mm(t, p_=pp, w_=wbuf[par], x_=xsrc):
                    r = None
                    for kt in range(NT):
                        r = t.matmul(p_[:], w_[:, kt * 128:(kt + 1) * 128],
                                     x_[:, kt * RK:(kt + 1) * RK],
                                     start=(kt == 0), stop=(kt == NT - 1))
                    return r
                mk = B.emit("t", B.dep_r("wb%d" % par, xname) + B.dep_w(pn), mm, "T")
                B.did_w(mk, pn); B.did_r(mk, "wb%d" % par, xname)
                if mt + 2 < NT:
                    dma_wtile(wsrc, mt + 2, wbuf, "wb", "DW")
                mk = B.emit("s", B.dep_r(pn) + B.dep_w(dstname + "_t%d" % mt),
                            lambda s, m_=mt, p_=pp, d_=dst: s.copy(d_[:, m_ * RK:(m_ + 1) * RK], p_[:]), "S")
                B.did_r(mk, pn); B.did_w(mk, dstname + "_t%d" % mt)

        def norms_front(srcbuf, srcname, cpar):
            nsem = "DN%d" % cpar
            for mt in range(NT):
                p = mt % 2
                mk = B.emit("v", B.dep_r(srcname + "_t%d" % mt) + B.dep_w("sqt%d" % p),
                            lambda v, m_=mt, p_=p, x_=srcbuf: v.tensor_tensor(
                                sqt[p_][:], x_[:, m_ * RK:(m_ + 1) * RK],
                                x_[:, m_ * RK:(m_ + 1) * RK], op=ALU.mult), "V")
                B.did_w(mk, "sqt%d" % p); B.did_r(mk, srcname + "_t%d" % mt)
                pn = "P%d" % (6 + p)
                mk = B.emit("t", B.dep_r("sqt%d" % p, "onesM") + B.dep_w(pn),
                            lambda t, p_=p: t.matmul(P[6 + p_][0:1, :], onesM[:], sqt[p_][:],
                                                     start=True, stop=True), "T")
                B.did_w(mk, pn); B.did_r(mk, "sqt%d" % p, "onesM")
                mk = B.emit("s", B.dep_r(pn) + B.dep_w("nbuf%d" % p),
                            lambda s, p_=p: s.copy(nbuf[p_][0:1, :], P[6 + p_][0:1, :]), "S")
                B.did_r(mk, pn); B.did_w(mk, "nbuf%d" % p)
                mk = B.emit("q", B.dep_r("nbuf%d" % p) + B.dep_w("nscr%d" % cpar),
                            lambda q, m_=mt, p_=p, c_=cpar: q.dma_start(
                                out=nscr[c_, m_].unsqueeze(0), in_=nbuf[p_][0:1, :]), nsem)
                B.did_r(mk, "nbuf%d" % p)
                if mt == NT - 1:
                    B.did_w(mk, "nscr%d" % cpar)
                else:
                    B.did_w(mk, "nscr_part%d" % cpar)

        def norms_back(srcbuf, srcname, cpar, sq_scale, maskname):
            nsem = "DN%d" % cpar
            rsem = "DR%d" % cpar
            mk = B.emit("q", [(nsem, B.cnt[nsem])] + B.dep_w("n16"),
                        lambda q, c_=cpar: q.dma_start(out=n16[:], in_=nscr[c_]), nsem)
            B.did_w(mk, "n16"); B.did_r(mk, "nscr%d" % cpar)
            mk = B.emit("s", B.dep_r("n16") + B.dep_w("rn32"),
                        lambda s: s.activation(rn32[:], n16[:], AF.Sqrt, scale=float(sq_scale)), "S")
            B.did_w(mk, "rn32"); B.did_r(mk, "n16")
            mk = B.emit("v", B.dep_r("rn32"),
                        lambda v: v.reciprocal(rn32[:], rn32[:]), "V")
            B.did_w(mk, "rn32")
            if maskname is not None:
                mi = int(maskname[-1])
                mk = B.emit("v", B.dep_r("rn32", maskname) + B.dep_w("rnm"),
                            lambda v, m_=mi: v.tensor_tensor(rnm[:], rn32[:], maskb[m_][:], op=ALU.mult), "V")
                B.did_r(mk, "rn32", maskname)
            else:
                mk = B.emit("v", B.dep_r("rn32") + B.dep_w("rnm"),
                            lambda v: v.tensor_copy(rnm[:], rn32[:]), "V")
                B.did_r(mk, "rn32")
            B.did_w(mk, "rnm")
            mk = B.emit("q", B.dep_r("rnm") + B.dep_w("rnscr%d" % cpar),
                        lambda q, c_=cpar: q.dma_start(out=rnscr[c_], in_=rnm[:]), rsem)
            B.did_r(mk, "rnm"); B.did_w(mk, "rnscr%d" % cpar)
            for mt in range(NT):
                p = mt % 2
                mk = B.emit("g", B.dep_r("rnscr%d" % cpar) + B.dep_w("rnb%d" % p),
                            lambda g, m_=mt, p_=p, c_=cpar: g.dma_start(
                                out=rnb[p_][:], in_=rnscr[c_, m_].unsqueeze(0).broadcast_to((128, RK))), rsem)
                B.did_w(mk, "rnb%d" % p); B.did_r(mk, "rnscr%d" % cpar)
                mk = B.emit("v", B.dep_r("rnb%d" % p, srcname + "_t%d" % mt),
                            lambda v, m_=mt, p_=p, x_=srcbuf: v.tensor_tensor(
                                x_[:, m_ * RK:(m_ + 1) * RK], x_[:, m_ * RK:(m_ + 1) * RK],
                                rnb[p_][:], op=ALU.mult), "V")
                B.did_w(mk, srcname + "_t%d" % mt); B.did_r(mk, "rnb%d" % p)

        def xslot(c):
            if c >= 6:
                return xn_own, "xn_own"
            return (xn_roll, "xn_roll") if c % 2 == 0 else (qsb, "qsb")

        def k_chunk(c):
            own = c >= 6
            cpar = c % 2
            xdst, xname = xslot(c)
            if c + 1 <= 6:
                dma_x_chunk(c + 1, *xslot(c + 1))
            dma_mask(c)
            ln_chunk(c, xdst, xname, own)
            proj(xdst, xname, wkv, kvb, "kvb", 0, 1)
            norms_front(kvb, "kvb", cpar)
            norms_back(kvb, "kvb", cpar, C_KV, "mkb%d" % cpar)
            cold, cnew = carry[c % 2], carry[1 - c % 2]
            coldn, cnewn = "carry%d" % (c % 2), "carry%d" % (1 - c % 2)
            for mt in range(NT):
                p = mt % 2
                mk = B.emit("s", B.dep_r("kvb_t%d" % mt) + B.dep_w("ctmp", "sqt%d" % p),
                            lambda s, m_=mt, p_=p: s.activation(
                                sqt[p_][:], kvb[:, m_ * RK:(m_ + 1) * RK], AF.Copy,
                                accum_out=ctmp[:, m_:m_ + 1]), "S")
                B.did_w(mk, "ctmp", "sqt%d" % p); B.did_r(mk, "kvb_t%d" % mt)
                mk = B.emit("g", B.dep_r("ctmp", coldn) + (B.dep_w(cnewn) if mt == 0 else []),
                            lambda g, m_=mt, co_=cold, cn_=cnew: g.tensor_tensor(
                                cn_[:, m_:m_ + 1], co_[:, m_:m_ + 1], ctmp[:, m_:m_ + 1], op=ALU.add), "G")
                B.did_r(mk, "ctmp", coldn)
                if mt == NT - 1:
                    B.did_w(mk, cnewn)

        def own_scans(c):
            cold = carry[c % 2]
            coldn = "carry%d" % (c % 2)
            for mt in range(NT):
                mk = B.emit("v", B.dep_r("kvb_t%d" % mt, coldn, "onesS"),
                            lambda v, m_=mt, co_=cold: v.tensor_tensor_scan(
                                kvb[:, m_ * RK:(m_ + 1) * RK], onesS[:],
                                kvb[:, m_ * RK:(m_ + 1) * RK], co_[:, m_:m_ + 1],
                                op0=ALU.mult, op1=ALU.add), "V")
                B.did_w(mk, "kvb_t%d" % mt)
                B.did_r(mk, coldn, "onesS")

        def q_chunk(oc):
            cpar = oc
            kname = "kvb"
            proj(xn_own, "xn_own", wq, qsb, "qsb", 0, 1)
            norms_front(qsb, "qsb", cpar)
            own_scans(6 + oc)
            norms_back(qsb, "qsb", cpar, C_Q, None)
            for mt in range(NT):
                p = mt % 2
                names = [f"hb{p}_{i}" for i in range(6)]
                pa, pb = P[2 * p], P[2 * p + 1]
                pan, pbn = "P%d" % (2 * p), "P%d" % (2 * p + 1)
                for (src, srcn, mat, matn, dsti) in (
                        (qsb, "qsb_t%d" % mt, cfb, "cfb", 0),
                        (qsb, "qsb_t%d" % mt, sfb, "sfb", 1),
                        (kvb, kname + "_t%d" % mt, cfb, "cfb", 2),
                        (kvb, kname + "_t%d" % mt, sfb, "sfb", 3)):
                    pp, pn = (pa, pan) if dsti % 2 == 0 else (pb, pbn)
                    mk = B.emit("t", B.dep_r(srcn, matn) + B.dep_w(pn),
                                lambda t, s_=src, m_=mt, w_=mat, p_=pp: t.matmul(
                                    p_[:], w_[:], s_[:, m_ * RK:(m_ + 1) * RK], start=True, stop=True), "T")
                    B.did_w(mk, pn); B.did_r(mk, srcn, matn)
                    mk = B.emit("s", B.dep_r(pn) + B.dep_w(names[dsti]),
                                lambda s, b_=hb[p][dsti], p_=pp: s.copy(b_[:], p_[:]), "S")
                    B.did_r(mk, pn); B.did_w(mk, names[dsti])
                def prods(v, b_=hb[p]):
                    QC, QS, KC, KS, T5, T6 = b_
                    v.tensor_tensor(T5[:], QC[:], KC[:], op=ALU.mult)
                    v.tensor_tensor(T6[:], QS[:], KS[:], op=ALU.mult)
                    v.tensor_tensor(T5[:], T5[:], T6[:], op=ALU.subtract)
                    v.tensor_tensor(T6[:], QC[:], KS[:], op=ALU.mult)
                    v.tensor_tensor(QC[:], QS[:], KC[:], op=ALU.mult)
                    return v.tensor_tensor(T6[:], T6[:], QC[:], op=ALU.add)
                mk = B.emit("v", B.dep_r(*names) + B.dep_w(names[0], names[4], names[5]), prods, "V")
                B.did_r(mk, *names); B.did_w(mk, names[0], names[4], names[5])
                mk = B.emit("t", B.dep_r(names[4], names[5], "cfb", "sfb") + B.dep_w("P4"),
                            lambda t, b_=hb[p]: (t.matmul(P[4][:], cfb[:], b_[4][:], start=True, stop=False),
                                                 t.matmul(P[4][:], sfb[:], b_[5][:], start=False, stop=True))[-1], "T")
                B.did_w(mk, "P4"); B.did_r(mk, names[4], names[5], "cfb", "sfb")
                mk = B.emit("s", B.dep_r("P4") + B.dep_w("vh_t%d" % mt),
                            lambda s, m_=mt: s.copy(vh[:, m_ * RK:(m_ + 1) * RK], P[4][:]), "S")
                B.did_r(mk, "P4"); B.did_w(mk, "vh_t%d" % mt)
            vh_names = ["vh_t%d" % k for k in range(NT)]
            dma_wtile(wo, 0, wbuf, "wb", "DW")
            dma_wtile(wo, 1, wbuf, "wb", "DW")
            for mt in range(NT):
                par = mt % 2
                pp, pn = P[6 + par], "P%d" % (6 + par)
                def mmo(t, p_=pp, w_=wbuf[par]):
                    r = None
                    for kt in range(NT):
                        r = t.matmul(p_[:], w_[:, kt * 128:(kt + 1) * 128],
                                     vh[:, kt * RK:(kt + 1) * RK], start=(kt == 0), stop=(kt == NT - 1))
                    return r
                mk = B.emit("t", B.dep_r("wb%d" % par, *vh_names) + B.dep_w(pn), mmo, "T")
                B.did_w(mk, pn); B.did_r(mk, "wb%d" % par, *vh_names)
                if mt + 2 < NT:
                    dma_wtile(wo, mt + 2, wbuf, "wb", "DW")
                mk = B.emit("s", B.dep_r(pn) + B.dep_w("ast0"),
                            lambda s, p_=pp, a_=attn_st[0]: s.copy(a_[:], p_[:]), "S")
                B.did_r(mk, pn); B.did_w(mk, "ast0")
                mk = B.emit("q", B.dep_r("ast0") + B.dep_w("ascr_t%d" % mt),
                            lambda q, m_=mt, a_=attn_st[0]: q.dma_start(
                                out=ascr[m_ * 128:(m_ + 1) * 128, :], in_=a_[:]), "DA%d" % par)
                B.did_r(mk, "ast0"); B.did_w(mk, "ascr_t%d" % mt)

        def f_chunk(oc):
            outcol = oc * RK
            dma_wtile(wg, 0, wgb, "wgb", "DG")
            dma_wtile(wu, 0, wub, "wub", "DU")
            dma_wtile(wg, 1, wgb, "wgb", "DG")
            dma_wtile(wu, 1, wub, "wub", "DU")
            for grp in range(4):
                for i in range(16):
                    mt = grp * 16 + i
                    par = mt % 2
                    pg, pgn = P[par], "P%d" % par
                    pu, pun = P[2 + par], "P%d" % (2 + par)
                    def mmg(t, p_=pg, w_=wgb[par]):
                        r = None
                        for kt in range(NT):
                            r = t.matmul(p_[:], w_[:, kt * 128:(kt + 1) * 128],
                                         xn_own[:, kt * RK:(kt + 1) * RK],
                                         start=(kt == 0), stop=(kt == NT - 1))
                        return r
                    mk = B.emit("t", B.dep_r("wgb%d" % par, "xn_own") + B.dep_w(pgn), mmg, "T")
                    B.did_w(mk, pgn); B.did_r(mk, "wgb%d" % par, "xn_own")
                    if mt + 2 < 64:
                        dma_wtile(wg, mt + 2, wgb, "wgb", "DG")
                    def mmu(t, p_=pu, w_=wub[par]):
                        r = None
                        for kt in range(NT):
                            r = t.matmul(p_[:], w_[:, kt * 128:(kt + 1) * 128],
                                         xn_own[:, kt * RK:(kt + 1) * RK],
                                         start=(kt == 0), stop=(kt == NT - 1))
                        return r
                    mk = B.emit("t", B.dep_r("wub%d" % par, "xn_own") + B.dep_w(pun), mmu, "T")
                    B.did_w(mk, pun); B.did_r(mk, "wub%d" % par, "xn_own")
                    if mt + 2 < 64:
                        dma_wtile(wu, mt + 2, wub, "wub", "DU")
                    mk = B.emit("s", B.dep_r(pgn, "sgv") + B.dep_w("silg%d" % par),
                                lambda s, m_=mt, p_=pg, g_=silg[par]: s.activation(
                                    g_[:], p_[:], AF.Silu, scale=sgv[:, m_:m_ + 1]), "S")
                    B.did_r(mk, pgn, "sgv"); B.did_w(mk, "silg%d" % par)
                    mk = B.emit("v", B.dep_r(pun, "suv", "silg%d" % par) + B.dep_w("fuse_t%d" % i),
                                lambda v, m_=mt, i_=i, p_=pu, g_=silg[par]: v.scalar_tensor_tensor(
                                    fuse[:, i_ * RK:(i_ + 1) * RK], p_[:], suv[:, m_:m_ + 1], g_[:],
                                    op0=ALU.mult, op1=ALU.mult), "V")
                    B.did_r(mk, pun, "suv", "silg%d" % par); B.did_w(mk, "fuse_t%d" % i)
                if grp == 0:
                    dma_wd_piece(0, grp, 0)
                    dma_wd_piece(1, grp, 1)
                for ot in range(NT):
                    dpar = ot % 2
                    pd, pdn = P[4 + dpar], "P%d" % (4 + dpar)
                    fdeps = [B.dep_r("fuse_t%d" % i)[0] for i in range(16)]
                    deps = B.dep_r("wdb%d" % dpar) + fdeps + B.dep_w(pdn)
                    def mmd(t, p_=pd, w_=wdb[dpar]):
                        r = None
                        for i in range(16):
                            r = t.matmul(p_[:], w_[:, i * 128:(i + 1) * 128],
                                         fuse[:, i * RK:(i + 1) * RK],
                                         start=(i == 0), stop=(i == 15))
                        return r
                    mk = B.emit("t", deps, mmd, "T")
                    B.did_w(mk, pdn)
                    B.did_r(mk, "wdb%d" % dpar, *["fuse_t%d" % i for i in range(16)])
                    nidx = grp * 16 + ot + 2
                    if nidx < 64:
                        dma_wd_piece(nidx, nidx // 16, (nidx % 16) and None or None) if False else None
                    if ot + 2 < NT:
                        dma_wd_piece(ot + 2, grp, dpar)
                    elif grp + 1 < 4:
                        dma_wd_piece(ot + 2 - NT, grp + 1, dpar)
                    if grp == 0:
                        mk = B.emit("s", B.dep_r(pdn) + B.dep_w("acc_t%d" % ot),
                                    lambda s, o_=ot, p_=pd: s.copy(acc[:, o_ * RK:(o_ + 1) * RK], p_[:]), "S")
                    else:
                        mk = B.emit("v", B.dep_r(pdn, "acc_t%d" % ot),
                                    lambda v, o_=ot, p_=pd: v.tensor_tensor(
                                        acc[:, o_ * RK:(o_ + 1) * RK], acc[:, o_ * RK:(o_ + 1) * RK],
                                        p_[:], op=ALU.add), "V")
                    B.did_r(mk, pdn); B.did_w(mk, "acc_t%d" % ot)
            for ot in range(NT):
                par = ot % 2
                mk = B.emit("q", B.dep_w("ard%d" % par) + B.dep_r("ascr_t%d" % ot),
                            lambda q, o_=ot, a_=a_rd[par]: q.dma_start(
                                out=a_[:], in_=ascr[o_ * 128:(o_ + 1) * 128, :]), "DB%d" % par)
                B.did_w(mk, "ard%d" % par); B.did_r(mk, "ascr_t%d" % ot)
                mk = B.emit("g", B.dep_w("xrd%d" % par),
                            lambda g, o_=ot, x_=x_rd[par]: g.dma_start(
                                out=x_[:], in_=xkv[o_ * 128:(o_ + 1) * 128, 3072 + outcol: 3072 + outcol + RK]),
                            "DB%d" % par)
                B.did_w(mk, "xrd%d" % par)
                mk = B.emit("v", B.dep_r("ard%d" % par, "xrd%d" % par) + B.dep_w("axr%d" % par),
                            lambda v, a_=a_rd[par], x_=x_rd[par], r_=axr[par]: v.tensor_tensor(
                                r_[:], a_[:], x_[:], op=ALU.add), "V")
                B.did_r(mk, "ard%d" % par, "xrd%d" % par); B.did_w(mk, "axr%d" % par)
                mk = B.emit("v", B.dep_r("acc_t%d" % ot, "sdv", "axr%d" % par) + B.dep_w("ost%d" % par),
                            lambda v, o_=ot, r_=axr[par], u_=outst[par]: v.scalar_tensor_tensor(
                                u_[:], acc[:, o_ * RK:(o_ + 1) * RK], sdv[:, o_:o_ + 1], r_[:],
                                op0=ALU.mult, op1=ALU.add), "V")
                B.did_r(mk, "acc_t%d" % ot, "sdv", "axr%d" % par); B.did_w(mk, "ost%d" % par)
                mk = B.emit("q", B.dep_r("ost%d" % par),
                            lambda q, o_=ot, u_=outst[par]: q.dma_start(
                                out=out[o_ * 128:(o_ + 1) * 128, outcol:outcol + RK], in_=u_[:]), "DO")
                B.did_r(mk, "ost%d" % par)

        dma_x_chunk(0, *xslot(0))
        for c in range(6):
            k_chunk(c)
        k_chunk(6)
        q_chunk(0)
        f_chunk(0)
        dma_x_chunk(7, *xslot(7))
        k_chunk(7)
        q_chunk(1)
        f_chunk(1)
        B.emit("q", [("DO", B.cnt["DO"])], lambda q: q.nop(), "G")
        B.flush()
    return nc


_NC_CACHE = None


def kernel(L0_attn_x, L0_attn_W_q, L0_attn_W_kv, L0_attn_W_o,
           L0_ffn_w_gate, L0_ffn_w_up, L0_ffn_w_down,
           L0_ffn_s_gate, L0_ffn_s_up, L0_ffn_s_down):
    x = np.asarray(L0_attn_x, dtype=np.float32)
    Wq = np.asarray(L0_attn_W_q, dtype=np.float32)
    Wkv = np.asarray(L0_attn_W_kv, dtype=np.float32)
    Wo = np.asarray(L0_attn_W_o, dtype=np.float32)
    wg = np.asarray(L0_ffn_w_gate, dtype=np.int8)
    wu = np.asarray(L0_ffn_w_up, dtype=np.int8)
    wd = np.asarray(L0_ffn_w_down, dtype=np.int8)
    sg = np.asarray(L0_ffn_s_gate, dtype=np.float32)
    su = np.asarray(L0_ffn_s_up, dtype=np.float32)
    sdn = np.asarray(L0_ffn_s_down, dtype=np.float32)

    t4a = lambda W: np.ascontiguousarray(W.reshape(16, 128, 16, 128).transpose(0, 3, 2, 1))
    wq_t, wkv_t, wo_t = t4a(Wq), t4a(Wkv), t4a(Wo)
    wg_t = np.ascontiguousarray(wg.reshape(16, 128, 64, 128).transpose(2, 1, 0, 3))
    wu_t = np.ascontiguousarray(wu.reshape(16, 128, 64, 128).transpose(2, 1, 0, 3))
    wd_t = np.ascontiguousarray(wd.reshape(64, 128, 16, 128).transpose(2, 1, 0, 3))
    sg_t = np.ascontiguousarray(sg.reshape(64, 128).T)
    su_t = np.ascontiguousarray(su.reshape(64, 128).T)
    sd_t = np.ascontiguousarray(sdn.reshape(16, 128).T)
    n = np.arange(128)
    ang = 2.0 * np.pi * np.outer(n, n) / 128.0
    cf = np.cos(ang).astype(np.float32)
    sf = np.sin(ang).astype(np.float32)

    in_maps = []
    for c in range(8):
        b, j = c // 4, c % 4
        xb = x[b]
        rolled = np.concatenate([xb[0:1024]] * (3 - j) + [xb[0:(j + 1) * 1024]], axis=0)
        x_kv = np.ascontiguousarray(rolled.T)
        m = np.zeros((16, SEQ), dtype=np.float32)
        m[:, (3 - j) * 1024:] = 1.0
        in_maps.append({
            "xkv": x_kv, "wq": wq_t, "wkv": wkv_t, "wo": wo_t,
            "wg": wg_t, "wu": wu_t, "wd": wd_t,
            "sg": sg_t, "su": su_t, "sd": sd_t,
            "cf": cf, "sf": sf, "mask": m,
        })

    global _NC_CACHE
    if _NC_CACHE is None:
        _NC_CACHE = build_program()
    res = run_bass_kernel_spmd(_NC_CACHE, in_maps, core_ids=list(range(8)))
    outp = np.empty((2, SEQ, D), dtype=np.float32)
    for c in range(8):
        b, j = c // 4, c % 4
        outp[b, j * 1024:(j + 1) * 1024, :] = np.asarray(res.results[c]["out"]).T
    return outp
